# version 13
# baseline (speedup 1.0000x reference)
import math
from contextlib import ExitStack

import numpy as np
import ml_dtypes
import concourse.bass as bass
import concourse.mybir as mybir
from concourse.bass_utils import run_bass_kernel_spmd
from concourse.aot_env import allow_ici_all_to_all

F32 = mybir.dt.float32
BF16 = mybir.dt.bfloat16
AF = mybir.ActivationFunctionType
ALU = mybir.AluOpType
AX = mybir.AxisListType

D = 2048
L_INIT = 0.8 - 0.6 * math.exp(-0.3 * 1)
C_SCALE = 1.0 - L_INIT

C_S16, C_BA, C_SF, C_BF, C_EPS6, C_I32, C_EPS5, C_CS, C_SQ2048, C_1E9, \
    C_M1E9, C_M1, C_EPS12, C_LI = range(14)
NCONST = 14


def build(S):
    NT = S // 128
    TC = S // 1024
    QT = S // 512
    MY = NT // 8
    TPC = S // 8
    G = 4 if MY % 4 == 0 else 1
    NG = MY // G

    nc = bass.Bass("TRN2", target_bir_lowering=False)

    xT = nc.dram_tensor("xT", [16, 128, S], F32, kind="ExternalInput")
    x_rows = nc.dram_tensor("x_rows", [MY, 128, D], F32, kind="ExternalInput")
    wqk = nc.dram_tensor("wqk", [16, 128, 1024], F32, kind="ExternalInput")
    wv = nc.dram_tensor("wv", [16, 128, 256], F32, kind="ExternalInput")
    lv = nc.dram_tensor("lv", [1, 32], F32, kind="ExternalInput")
    woT = nc.dram_tensor("woT", [16, 128, D], F32, kind="ExternalInput")
    simm = nc.dram_tensor("simm", [16, 128, 16], F32, kind="ExternalInput")
    gates = nc.dram_tensor("gates", [1, 16], F32, kind="ExternalInput")
    qall = nc.dram_tensor("qall", [128, 4, 16, 512], F32, kind="ExternalInput")
    kall = nc.dram_tensor("kall", [128, 4, 16, 512], F32, kind="ExternalInput")
    vall = nc.dram_tensor("vall", [128, 4, 16, 512], F32, kind="ExternalInput")
    oall = nc.dram_tensor("oall", [128, 4, 16, 512], F32, kind="ExternalInput")
    fmask = nc.dram_tensor("fmask", [128, 1024], BF16, kind="ExternalInput")

    out_rows = nc.dram_tensor("out_rows", [MY, 128, D], F32, kind="ExternalOutput")

    qt_hbm = nc.dram_tensor("qt_hbm", [4, 128, S], BF16)
    kt_hbm = nc.dram_tensor("kt_hbm", [4, 128, S], BF16)
    v_hbm = nc.dram_tensor("v_hbm", [NT, 128, 256], BF16)
    y_hbm = nc.dram_tensor("y_hbm", [MY, 128, D], F32)
    cc_in = nc.dram_tensor("cc_in", [8, 256, TPC], BF16)
    cc_out = nc.dram_tensor("cc_out", [8, 256, TPC], BF16)
    cc2_in = nc.dram_tensor("cc2_in", [2, 128, MY * 128], BF16)
    cc2_out = nc.dram_tensor("cc2_out", [8, 2, 128, MY * 128], BF16)

    cvals = np.zeros((128, NCONST), dtype=np.float32)
    cvals[:, C_S16] = 1.0 / 16.0
    cvals[:, C_BA] = -4.0
    cvals[:, C_SF] = 1.0 / math.sqrt(128.0)
    cvals[:, C_BF] = -2.0
    cvals[:, C_EPS6] = 1e-6
    cvals[:, C_I32] = 1.0 / 32.0
    cvals[:, C_EPS5] = 1e-5
    cvals[:, C_CS] = C_SCALE
    cvals[:, C_SQ2048] = math.sqrt(2048.0)
    cvals[:, C_1E9] = 1e9
    cvals[:, C_M1E9] = -1e9
    cvals[:, C_M1] = -1.0
    cvals[:, C_EPS12] = 1e-12
    cvals[:, C_LI] = L_INIT
    cst_d = nc.inline_tensor(cvals, "cst_d")
    ident_d = nc.inline_tensor(np.eye(128, dtype=np.float32), "ident_d")
    dmv = np.zeros((128, 896), dtype=np.float32)
    for p in range(128):
        dmv[p, 384 + p:896] = 1.0
    dm_d = nc.inline_tensor(dmv, "dm_d")

    class Sem:
        def __init__(self, name):
            self.h = nc.alloc_semaphore(name)
            self.n = 0

        def inc(self, inst, k=1):
            inst.then_inc(self.h, k)
            self.n += k
            return self.n

    def W(eng, s, v):
        if v > 0:
            eng.wait_ge(s.h, v)

    def bar():
        nc.all_engine_barrier()

    sQ = Sem("sQ")
    sG = Sem("sG")

    def dma_b(out_ap, in_ap):
        sQ.inc(nc.sync.dma_start(out=out_ap, in_=in_ap), 16)

    def dma_b_wait():
        nc.sync.wait_ge(sQ.h, sQ.n)

    def gdma_b(out_ap, in_ap):
        sG.inc(nc.gpsimd.dma_start(out=out_ap, in_=in_ap), 16)

    def gdma_b_wait():
        nc.gpsimd.wait_ge(sG.h, sG.n)

    vch = Sem("vch")
    sCH = Sem("sCH")

    def CI(inst):
        sCH.inc(inst)
        return inst

    def CW(eng):
        eng.wait_ge(sCH.h, sCH.n)

    def VS(inst):
        vch.inc(inst)
        nc.vector.wait_ge(vch.h, vch.n)

    st_base = ExitStack()
    cst = st_base.enter_context(nc.sbuf_tensor("cst", [128, NCONST], F32))
    idf = st_base.enter_context(nc.sbuf_tensor("idf", [128, 128], F32))
    onesf = st_base.enter_context(nc.sbuf_tensor("onesf", [1, 128], F32))
    onesc = st_base.enter_context(nc.sbuf_tensor("onesc", [128, 1], F32))
    onesb = st_base.enter_context(nc.sbuf_tensor("onesb", [128, 1], BF16))
    lam = st_base.enter_context(nc.sbuf_tensor("lam", [128, 1], F32))
    lamn = st_base.enter_context(nc.sbuf_tensor("lamn", [128, 1], F32))
    w_sb = st_base.enter_context(nc.sbuf_tensor("w_sb", [128, MY, 16], F32))
    qTf = st_base.enter_context(nc.sbuf_tensor("qTf", [128, MY, 128], BF16))
    smallf = st_base.enter_context(nc.sbuf_tensor("smallf", [128, 56], F32))
    woT_bf = st_base.enter_context(nc.sbuf_tensor("woT_bf", [128, 16, D], BF16))
    PSA = st_base.enter_context(nc.psum_tensor("PSA", [128, 1024], F32))
    PSB = st_base.enter_context(nc.psum_tensor("PSB", [128, 1024], F32))
    PSC = st_base.enter_context(nc.psum_tensor("PSC", [128, 1024], F32))
    PSD = st_base.enter_context(nc.psum_tensor("PSD", [128, 512], F32))
    PSP = st_base.enter_context(nc.psum_tensor("PSP", [128, 512], F32))

    C = lambda i: cst[:, i:i + 1]
    C0 = lambda i: cst[0:1, i:i + 1]

    dma_b(cst[:, :], cst_d[:, :])
    dma_b(idf[:, :], ident_d[:, :])
    dma_b(smallf[0:1, 0:32], lv[:, :])
    dma_b_wait()
    bar()
    nc.vector.memset(onesf[:, :], 1.0)
    nc.vector.memset(onesc[:, :], 1.0)
    nc.vector.memset(onesb[:, :], 1.0)
    nc.vector.tensor_mul(smallf[0:1, 32:40], smallf[0:1, 0:8], smallf[0:1, 8:16])
    nc.vector.tensor_mul(smallf[0:1, 40:48], smallf[0:1, 16:24], smallf[0:1, 24:32])
    bar()
    nc.vector.reduce_sum(smallf[0:1, 48:49], smallf[0:1, 32:40], AX.X)
    nc.vector.reduce_sum(smallf[0:1, 49:50], smallf[0:1, 40:48], AX.X)
    bar()
    nc.scalar.activation(smallf[0:1, 50:52], smallf[0:1, 48:50], AF.Exp)
    bar()
    nc.vector.tensor_sub(smallf[0:1, 52:53], smallf[0:1, 50:51], smallf[0:1, 51:52])
    bar()
    nc.vector.tensor_scalar_add(smallf[0:1, 53:54], smallf[0:1, 52:53], C0(C_LI))
    bar()
    nc.tensor.matmul(PSD[:, 0:1], onesf[:, :], smallf[0:1, 53:54], start=True, stop=True)
    bar()
    nc.vector.tensor_copy(lam[:, :], PSD[:, 0:1])
    bar()
    nc.vector.tensor_scalar_mul(lamn[:, :], lam[:, :], C(C_M1))
    bar()
    stA = ExitStack()
    wqk_bf = stA.enter_context(nc.sbuf_tensor("wqk_bf", [128, 16, 1024], BF16))
    wv_bf = stA.enter_context(nc.sbuf_tensor("wv_bf", [128, 16, 256], BF16))

    sWA = [Sem("sWA0"), Sem("sWA1")]
    sWAc = Sem("sWAc")
    wst2 = stA.enter_context(nc.sbuf_tensor("wst2", [128, 2, 1280], F32))
    for dc in range(16):
        W(nc.gpsimd, sWAc, dc - 1)
        sWA[dc % 2].inc(nc.gpsimd.dma_start(
            out=wst2[:, dc % 2, 0:1024], in_=wqk[dc, :, :]), 16)
        sWA[dc % 2].inc(nc.gpsimd.dma_start(
            out=wst2[:, dc % 2, 1024:1280], in_=wv[dc, :, :]), 16)
        W(nc.scalar, sWA[dc % 2], 32 * (dc // 2 + 1))
        nc.scalar.copy(wqk_bf[:, dc, :], wst2[:, dc % 2, 0:1024])
        sWAc.inc(nc.scalar.copy(wv_bf[:, dc, :], wst2[:, dc % 2, 1024:1280]))
    xbf = stA.enter_context(nc.sbuf_tensor("xbf", [128, 2, 16, 1024], BF16))
    xst = stA.enter_context(nc.sbuf_tensor("xst", [128, 2, 1024], F32))
    kq_st = stA.enter_context(nc.sbuf_tensor("kq_st", [128, 2, 1024], BF16))
    v_st = stA.enter_context(nc.sbuf_tensor("v_st", [128, 2, 256], BF16))

    sXd = [Sem("sXd0"), Sem("sXd1")]
    sXc = Sem("sXc")
    sTq = Sem("sTq")
    sTv = Sem("sTv")
    sVq = Sem("sVq")
    sVv = Sem("sVv")
    sGq = [Sem("sGq0"), Sem("sGq1")]
    sGv = [Sem("sGv0"), Sem("sGv1")]

    for tc in range(TC):
        for dc in range(16):
            i = tc * 16 + dc
            W(nc.sync, sXc, i - 1)
            sXd[i % 2].inc(nc.sync.dma_start(
                out=xst[:, i % 2, :], in_=xT[dc, :, tc * 1024:(tc + 1) * 1024]), 16)
            W(nc.scalar, sXd[i % 2], 16 * (i // 2 + 1))
            if tc >= 2:
                W(nc.scalar, sTq, 8 * (tc - 1))
                W(nc.scalar, sTv, 8 * (tc - 1))
            sXc.inc(nc.scalar.copy(xbf[:, tc % 2, dc, :], xst[:, i % 2, :]))
        for qc in range(8):
            gg = tc * 8 + qc
            if tc == 0 and qc == 0:
                W(nc.tensor, sWAc, 16)
            W(nc.tensor, sXc, 16 * (tc + 1))
            W(nc.tensor, sVq, gg - 1)
            PS = PSA if gg % 2 == 0 else PSB
            mm = None
            for th in range(2):
                for dc in range(16):
                    mm = nc.tensor.matmul(
                        PS[:, th * 512:(th + 1) * 512],
                        wqk_bf[:, dc, qc * 128:(qc + 1) * 128],
                        xbf[:, tc % 2, dc, th * 512:(th + 1) * 512],
                        start=(dc == 0), stop=(dc == 15))
            sTq.inc(mm)
            W(nc.vector, sTq, gg + 1)
            W(nc.vector, sGq[gg % 2], 16 * (gg // 2))
            sVq.inc(nc.vector.tensor_copy(kq_st[:, gg % 2, :], PS[:, :]))
            W(nc.gpsimd, sVq, gg + 1)
            dst = qt_hbm if qc < 4 else kt_hbm
            sGq[gg % 2].inc(nc.gpsimd.dma_start(
                out=dst[qc % 4, :, tc * 1024:(tc + 1) * 1024],
                in_=kq_st[:, gg % 2, :]), 16)
        for vt in range(8):
            gv = tc * 8 + vt
            if tc == 0 and vt == 0:
                W(nc.tensor, sWAc, 16)
            W(nc.tensor, sXc, 16 * (tc + 1))
            W(nc.tensor, sVv, gv - 1)
            half = (gv % 2) * 512
            mm = None
            for dc in range(16):
                mm = nc.tensor.matmul(
                    PSC[:, half:half + 256],
                    xbf[:, tc % 2, dc, vt * 128:(vt + 1) * 128],
                    wv_bf[:, dc, :],
                    start=(dc == 0), stop=(dc == 15))
            sTv.inc(mm)
            W(nc.vector, sTv, gv + 1)
            W(nc.vector, sGv[gv % 2], 16 * (gv // 2))
            sVv.inc(nc.vector.tensor_copy(v_st[:, gv % 2, :], PSC[:, half:half + 256]))
            W(nc.gpsimd, sVv, gv + 1)
            sGv[gv % 2].inc(nc.gpsimd.dma_start(
                out=v_hbm[tc * 8 + vt, :, :], in_=v_st[:, gv % 2, :]), 16)
    for s_ in sGq + sGv:
        nc.gpsimd.wait_ge(s_.h, s_.n)
    stA.close()
    bar()
    stB = ExitStack()
    KT = stB.enter_context(nc.sbuf_tensor("KT", [128, 4, S], BF16))
    Vt = stB.enter_context(nc.sbuf_tensor("Vt", [128, NT, 256], BF16))
    qt_sb = stB.enter_context(nc.sbuf_tensor("qt_sb", [128, 1, 4, 512], BF16))
    pt_sb = stB.enter_context(nc.sbuf_tensor("pt_sb", [128, 7, 1024], BF16))
    dm_sb = stB.enter_context(nc.sbuf_tensor("dm_sb", [128, 896], BF16))
    wstg = stB.enter_context(nc.sbuf_tensor("wstg", [128, 2, 512], F32))
    rpack = stB.enter_context(nc.sbuf_tensor("rpack", [1, 1024], F32))
    rrsb = rpack[0:1, :]
    rbc_sb = stB.enter_context(nc.sbuf_tensor("rbc_sb", [128, 1024], F32))

    oca = stB.enter_context(nc.sbuf_tensor("oca", [128, 512], F32))
    ocb = stB.enter_context(nc.sbuf_tensor("ocb", [128, 512], F32))
    sq4_sb = stB.enter_context(nc.sbuf_tensor("sq4_sb", [128, 1024], F32))
    obt4 = sq4_sb
    gs_sb = stB.enter_context(nc.sbuf_tensor("gs_sb", [128, 32], F32))
    sqr_sb = stB.enter_context(nc.sbuf_tensor("sqr_sb", [128, 32], F32))
    rs_sb = stB.enter_context(nc.sbuf_tensor("rs_sb", [128, 32], F32))
    obf4 = stB.enter_context(nc.sbuf_tensor("obf4", [128, 1024], BF16))
    otmp = obf4.ap().bitcast(F32)

    pass
    for ch in range(4):
        dma_b(KT[:, ch, :], kt_hbm[ch, :, :])
    for n in range(NT):
        dma_b(Vt[:, n, :], v_hbm[n, :, :])
    dma_b_wait()
    bar()
    for hh in range(2):
        n0, n1 = hh * 448, (hh + 1) * 448
        dma_b(wstg.ap().rearrange("p a b -> p (a b)")[:, 0:448], dm_d[:, n0:n1])
        dma_b_wait()
        bar()
        nc.vector.tensor_copy(dm_sb[:, n0:n1],
                              wstg.ap().rearrange("p a b -> p (a b)")[:, 0:448])
        bar()

    sQt = [Sem("sQt0"), Sem("sQt1")]
    sTS = Sem("sTS")
    sEx = Sem("sEx")
    sMk = Sem("sMk")
    sTt = Sem("sTt")
    sVr = Sem("sVr")
    sTr = Sem("sTr")
    sVb = Sem("sVb")
    sVc = Sem("sVc")
    sTf = Sem("sTf")
    sSq = Sem("sSq")
    sVg = Sem("sVg")
    sSr = Sem("sSr")
    sVo = Sem("sVo")
    sTb = Sem("sTb")
    sVf = Sem("sVf")
    sGo = Sem("sGo")
    sTP = Sem("sTP")

    sWoD = [Sem("sWoD0"), Sem("sWoD1")]
    sWoC = Sem("sWoC")

    def emit_wot(i):
        dc, h = i // 4, i % 4
        W(nc.sync, sWoC, i - 1)
        sWoD[i % 2].inc(nc.sync.dma_start(
            out=wstg[:, i % 2, :], in_=woT[dc, :, h * 512:(h + 1) * 512]), 16)
        W(nc.scalar, sWoD[i % 2], 16 * (i // 2 + 1))
        sWoC.inc(nc.scalar.copy(woT_bf[:, dc, h * 512:(h + 1) * 512],
                                wstg[:, i % 2, :]))

    KB = 0
    DD = 5
    NSLOT = 7
    mkcount = {}
    epi = {}
    for t in range(QT):
        nk = 4 * t + 4
        if t >= 1:
            W(nc.sync, sTS, 2 * t * (t + 1))
        for ch in range(4):
            sQt[0].inc(nc.sync.dma_start(
                out=qt_sb[:, 0, ch, :],
                in_=qt_hbm[ch, :, t * 512:(t + 1) * 512]), 16)
        for i in range(4 * t, min(4 * t + 4, 64)):
            emit_wot(i)

        def emit_pv(j, first, last, t=t, nk=nk):
            if j >= nk - 4:
                W(nc.tensor, sMk, mkcount[(t, j)])
            if j == 0:
                W(nc.tensor, sVo, t)
                W(nc.tensor, sVf, t)
                W(nc.tensor, sVb, t)
                W(nc.tensor, sVr, t)
            for dv in range(2):
                PO = PSB if dv == 0 else PSC
                for sm in range(2):
                    nc.tensor.matmul(
                        PO[:, sm * 512:(sm + 1) * 512],
                        Vt[:, j, dv * 128:(dv + 1) * 128],
                        pt_sb[:, j % NSLOT, sm * 512:(sm + 1) * 512],
                        start=first, stop=last)
            nc.tensor.matmul(PSD[0:1, 0:512], onesb[:, :],
                             pt_sb[:, j % NSLOT, 0:512], start=first, stop=last)
            mm = nc.tensor.matmul(PSP[0:1, 0:512], onesb[:, :],
                                  pt_sb[:, j % NSLOT, 512:1024], start=first, stop=last)
            sTP.inc(mm)

        for k in range(nk):
            g = KB + k
            W(nc.tensor, sEx, g)
            if k == 0:
                W(nc.tensor, sQt[0], 64 * (t + 1))
            for sm in range(2):
                for dh in range(2):
                    ch = sm * 2 + dh
                    mm = nc.tensor.matmul(
                        PSA[:, sm * 512:(sm + 1) * 512],
                        KT[:, ch, k * 128:(k + 1) * 128],
                        qt_sb[:, 0, ch, :],
                        start=(dh == 0), stop=(dh == 1))
            sTS.inc(mm)
            if k == 1 and "rbc" in epi:
                epi.pop("rbc")()
            if k == 2 and "fwd" in epi:
                epi.pop("fwd")()
            if k == 5 and "bwd" in epi:
                epi.pop("bwd")()
            if k >= DD:
                emit_pv(k - DD, first=(k - DD == 0), last=False)
            W(nc.scalar, sTS, g + 1)
            W(nc.scalar, sTP, KB + max(0, k - (NSLOT - 1)))
            sEx.inc(nc.scalar.activation(
                pt_sb[:, k % NSLOT, :], PSA[:, :], AF.Exp,
                bias=C(C_BA), scale=C(C_S16)))
            if k == 2 and "sq" in epi:
                epi.pop("sq")()
            if k == 3 and "sr" in epi:
                epi.pop("sr")()
            if k >= nk - 4:
                i = k - (nk - 4)
                W(nc.vector, sEx, g + 1)
                sMk.inc(nc.vector.tensor_mul(
                    pt_sb[:, k % NSLOT, :].rearrange("p (s q) -> p s q", s=2),
                    pt_sb[:, k % NSLOT, :].rearrange("p (s q) -> p s q", s=2),
                    dm_sb[:, 384 - 128 * i:896 - 128 * i]
                    .unsqueeze(1).broadcast_to([128, 2, 512])))
                mkcount[(t, k)] = sMk.n
        for nm in ("rbc", "fwd", "sq", "sr", "bwd"):
            if nm in epi:
                epi.pop(nm)()
        for j in range(max(0, nk - DD), nk):
            emit_pv(j, first=(j == 0), last=(j == nk - 1))
        KB += nk

        W(nc.vector, sTP, KB)
        CI(nc.vector.reciprocal(rrsb[:, 0:512], PSD[0:1, 0:512]))
        CI(nc.vector.reciprocal(rrsb[:, 512:1024], PSP[0:1, 0:512]))
        CW(nc.vector)
        sVr.inc(nc.vector.tensor_scalar_mul(
            rrsb[:, 512:1024], rrsb[:, 512:1024], lamn[0:1, :]))

        def pe_rbc(t=t):
            W(nc.tensor, sVr, t + 1)
            nc.tensor.matmul(PSD[:, 0:512], onesf[:, :], rrsb[:, 0:512],
                             start=True, stop=True)
            sTr.inc(nc.tensor.matmul(PSP[:, 0:512], onesf[:, :],
                                     rrsb[:, 512:1024], start=True, stop=True))

        W(nc.vector, sTr, t + 1)
        CI(nc.vector.tensor_copy(rbc_sb[:, 0:512], PSD[:, 0:512]))
        sVb.inc(nc.vector.tensor_copy(rbc_sb[:, 512:1024], PSP[:, 0:512]))
        W(nc.vector, sVb, t + 1)
        W(nc.vector, sGo, 128 * t)
        CI(nc.vector.tensor_mul(oca[:, :], PSB[:, 0:512], rbc_sb[:, 0:512]))
        CI(nc.vector.tensor_mul(otmp[:, :], PSB[:, 512:1024], rbc_sb[:, 512:1024]))
        CW(nc.vector)
        CI(nc.vector.tensor_add(oca[:, :], oca[:, :], otmp[:, :]))
        CI(nc.vector.tensor_mul(ocb[:, :], PSC[:, 0:512], rbc_sb[:, 0:512]))
        CW(nc.vector)
        CI(nc.vector.tensor_mul(otmp[:, :], PSC[:, 512:1024], rbc_sb[:, 512:1024]))
        CW(nc.vector)
        sVc.inc(nc.vector.tensor_add(ocb[:, :], ocb[:, :], otmp[:, :]))

        def pe_fwd(t=t):
            W(nc.tensor, sVc, t + 1)
            for qs in range(4):
                nc.tensor.transpose(PSB[:, qs * 256:qs * 256 + 128],
                                    oca[:, qs * 128:(qs + 1) * 128], idf[:, :])
                mm = nc.tensor.transpose(PSB[:, qs * 256 + 128:qs * 256 + 256],
                                         ocb[:, qs * 128:(qs + 1) * 128], idf[:, :])
            sTf.inc(mm)

        def sc_sq(t=t):
            W(nc.scalar, sTf, t + 1)
            sSq.inc(nc.scalar.activation(sq4_sb[:, :], PSB[:, :], AF.Square))

        W(nc.vector, sSq, t + 1)
        sVg.inc(nc.vector.reduce_sum(
            gs_sb[:, :], sq4_sb.ap().rearrange("p (g k) -> p g k", k=32), AX.X))
        def sc_sr(t=t):
            W(nc.scalar, sVg, t + 1)
            sSr.inc(nc.scalar.activation(sqr_sb[:, :], gs_sb[:, :], AF.Sqrt,
                                         bias=C(C_EPS6), scale=C(C_I32)))

        W(nc.vector, sSr, t + 1)
        VS(nc.vector.reciprocal(rs_sb[:, :], sqr_sb[:, :]))
        VS(nc.vector.tensor_scalar_mul(rs_sb[:, :], rs_sb[:, :], C(C_CS)))
        sVo.inc(nc.vector.tensor_tensor(
            obt4.ap().rearrange("p (g k) -> p g k", k=32),
            PSB.ap().rearrange("p (g k) -> p g k", k=32),
            rs_sb.ap().unsqueeze(2).broadcast_to([128, 32, 32]),
            ALU.mult))

        def pe_bwd(t=t):
            W(nc.tensor, sVo, t + 1)
            for qs in range(4):
                nc.tensor.transpose(PSC[:, qs * 256:qs * 256 + 128],
                                    obt4[:, qs * 256:qs * 256 + 128], idf[:, :])
                mm = nc.tensor.transpose(PSC[:, qs * 256 + 128:qs * 256 + 256],
                                         obt4[:, qs * 256 + 128:qs * 256 + 256], idf[:, :])
            sTb.inc(mm)

        W(nc.vector, sTb, t + 1)
        W(nc.vector, sGo, 128 * t)
        sVf.inc(nc.vector.tensor_copy(obf4[:, :], PSC[:, :]))
        W(nc.gpsimd, sVf, t + 1)
        for qs in range(4):
            b = t * 4 + qs
            shard, col = b % 8, (b // 8) * 128
            for a in range(2):
                sGo.inc(nc.gpsimd.dma_start(
                    out=cc_in[shard, a * 128:(a + 1) * 128, col:col + 128],
                    in_=obf4[:, qs * 256 + a * 128:qs * 256 + (a + 1) * 128]), 16)
        if t + 1 < QT:
            epi = {"rbc": pe_rbc, "fwd": pe_fwd, "sq": sc_sq, "sr": sc_sr,
                   "bwd": pe_bwd}
        else:
            pe_rbc()
            pe_fwd()
            sc_sq()
            sc_sr()
            pe_bwd()

    for i in range(4 * QT, 64):
        emit_wot(i)
    nc.scalar.wait_ge(sWoC.h, 64)
    nc.gpsimd.wait_ge(sGo.h, sGo.n)

    stB.close()
    bar()
    sCC = Sem("sCC")
    with allow_ici_all_to_all():
        sCC.inc(nc.gpsimd.collective_compute(
            "AllToAll", ALU.bypass, replica_groups=[list(range(8))],
            ins=[cc_in.ap().opt()], outs=[cc_out.ap().opt()]))
    nc.gpsimd.wait_ge(sCC.h, 1)
    bar()
    stD = ExitStack()
    ot_sb = stD.enter_context(nc.sbuf_tensor("ot_sb", [128, 2, 16, 128], BF16))
    x_sb = stD.enter_context(nc.sbuf_tensor("x_sb", [128, 2, D], F32))
    z_sb = stD.enter_context(nc.sbuf_tensor("z_sb", [128, D], F32))
    y_sb = stD.enter_context(nc.sbuf_tensor("y_sb", [128, D], F32))
    yT_sb = stD.enter_context(nc.sbuf_tensor("yT_sb", [128, 16, TPC], BF16))
    simn = stD.enter_context(nc.sbuf_tensor("simn", [128, 16, 16], F32))
    gb_sb = stD.enter_context(nc.sbuf_tensor("gb_sb", [128, 16], F32))
    rt = stD.enter_context(nc.sbuf_tensor("rt", [128, 160], F32))
    stg_big = stD.enter_context(nc.sbuf_tensor("stg_big", [128, 8, 512], F32))
    yTf_sb = stg_big.ap().rearrange("p a b -> p (a b)")[:, 0:2048]\
        .rearrange("p (a b) -> p a b", a=16)
    acc_q = stD.enter_context(nc.sbuf_tensor("acc_q", [128, MY, 128], F32))
    acc_k = stD.enter_context(nc.sbuf_tensor("acc_k", [128, MY, 128], F32))
    acc_v = stD.enter_context(nc.sbuf_tensor("acc_v", [128, MY, 128], F32))
    wp_bf = stD.enter_context(nc.sbuf_tensor("wp_bf", [128, 2, 16, 512], BF16))

    for dc in range(16):
        dma_b(simn[:, dc, :], simm[dc, :, :])
    dma_b(rt[0:1, 0:16], gates[:, :])
    dma_b_wait()
    bar()
    with nc.sbuf_tensor("simsq", [128, 16, 16], F32) as simsq:
        nc.vector.tensor_mul(simsq[:, :, :], simn[:, :, :], simn[:, :, :])
        bar()
        for dc in range(16):
            nc.tensor.matmul(PSD[0:1, 0:16], onesc[:, :], simsq[:, dc, :],
                             start=(dc == 0), stop=(dc == 15))
        bar()
    nc.vector.tensor_copy(rt[0:1, 48:64], PSD[0:1, 0:16])
    bar()
    nc.scalar.activation(rt[0:1, 64:80], rt[0:1, 48:64], AF.Sqrt)
    bar()
    nc.vector.reciprocal(rt[0:1, 80:96], rt[0:1, 64:80])
    nc.scalar.activation(rt[0:1, 96:112], rt[0:1, 0:16], AF.Sigmoid)
    bar()
    nc.tensor.matmul(PSD[:, 16:32], onesf[:, :], rt[0:1, 80:96], start=True, stop=True)
    nc.tensor.matmul(PSD[:, 32:48], onesf[:, :], rt[0:1, 96:112], start=True, stop=True)
    bar()
    nc.vector.tensor_copy(rt[:, 112:128], PSD[:, 16:32])
    nc.vector.tensor_copy(gb_sb[:, :], PSD[:, 32:48])
    bar()
    nc.vector.tensor_tensor(
        simn[:, :, :], simn[:, :, :],
        rt[:, 112:128].unsqueeze(1).broadcast_to([128, 16, 16]), ALU.mult)
    bar()

    sQo = [Sem("sQo0"), Sem("sQo1")]
    sQx = [Sem("sQx0"), Sem("sQx1")]
    sTwo = Sem("sTwo")
    zval = {}
    d1_pending = None
    for m in range(MY):
        if m == 0:
            for oc in range(16):
                sQo[0].inc(nc.sync.dma_start(
                    out=ot_sb[:, 0, oc, :],
                    in_=cc_out[oc // 2, (oc % 2) * 128:(oc % 2 + 1) * 128, 0:128]), 16)
            sQx[0].inc(nc.sync.dma_start(out=x_sb[:, 0, :], in_=x_rows[0, :, :]), 16)
        if m + 1 < MY:
            W(nc.sync, sTwo, m)
            for oc in range(16):
                sQo[(m + 1) % 2].inc(nc.sync.dma_start(
                    out=ot_sb[:, (m + 1) % 2, oc, :],
                    in_=cc_out[oc // 2, (oc % 2) * 128:(oc % 2 + 1) * 128,
                               (m + 1) * 128:(m + 2) * 128]), 16)
            if m >= 1:
                W(nc.sync, sCH, zval[m - 1])
            sQx[(m + 1) % 2].inc(nc.sync.dma_start(
                out=x_sb[:, (m + 1) % 2, :], in_=x_rows[m + 1, :, :]), 16)
        W(nc.tensor, sQo[m % 2], 256 * (m // 2 + 1))
        if m >= 1:
            W(nc.tensor, sCH, zval[m - 1])
        mm = None
        for oc in range(16):
            for cg in range(4):
                PS = PSA if cg < 2 else PSB
                mm = nc.tensor.matmul(
                    PS[:, (cg % 2) * 512:(cg % 2 + 1) * 512],
                    ot_sb[:, m % 2, oc, :],
                    woT_bf[:, oc, cg * 512:(cg + 1) * 512],
                    start=(oc == 0), stop=(oc == 15))
        sTwo.inc(mm)
        if d1_pending is not None:
            d1_pending()
            d1_pending = None
        W(nc.vector, sTwo, m + 1)
        W(nc.vector, sQx[m % 2], 16 * (m // 2 + 1))
        CI(nc.vector.tensor_add(z_sb[:, 0:1024], PSA[:, :], x_sb[:, m % 2, 0:1024]))
        CI(nc.vector.tensor_add(z_sb[:, 1024:2048], PSB[:, :],
                                x_sb[:, m % 2, 1024:2048]))
        zval[m] = sCH.n
        CW(nc.vector)
        for cch in range(4):
            CI(nc.vector.bn_stats(rt[:, 128 + cch * 6:134 + cch * 6],
                                  z_sb[:, cch * 512:(cch + 1) * 512]))
        CW(nc.vector)
        CI(nc.vector.bn_aggr(rt[:, 152:154], rt[:, 128:152]))
        CW(nc.scalar)
        CI(nc.scalar.activation(rt[:, 154:155], rt[:, 153:154], AF.Sqrt,
                                bias=C(C_EPS5)))
        CI(nc.scalar.activation(rt[:, 155:156], rt[:, 153:154], AF.Sqrt))
        CW(nc.vector)
        CI(nc.vector.reciprocal(rt[:, 156:157], rt[:, 154:155]))
        CW(nc.vector)
        CI(nc.vector.tensor_mul(rt[:, 157:158], rt[:, 152:153], rt[:, 156:157]))
        CI(nc.vector.tensor_mul(rt[:, 158:159], rt[:, 155:156], rt[:, 156:157]))
        CW(nc.vector)
        CI(nc.vector.tensor_scalar_mul(rt[:, 157:158], rt[:, 157:158], C(C_M1)))
        CI(nc.vector.tensor_scalar_mul(rt[:, 158:159], rt[:, 158:159], C(C_SQ2048)))
        CW(nc.vector)
        CI(nc.vector.tensor_scalar_max(rt[:, 158:159], rt[:, 158:159], C(C_EPS12)))
        CW(nc.vector)
        CI(nc.vector.reciprocal(rt[:, 159:160], rt[:, 158:159]))
        W(nc.vector, sG, sG.n)
        CW(nc.vector)
        CI(nc.vector.tensor_scalar(y_sb[:, :], z_sb[:, :], rt[:, 156:157],
                                   rt[:, 157:158], ALU.mult, ALU.add))
        CW(nc.gpsimd)
        gdma_b(y_hbm[m, :, :], y_sb[:, :])

        def pe_d1(m):
            CW(nc.tensor)
            for g4 in range(4):
                hb = (g4 % 2) * 512
                for i in range(4):
                    dc = g4 * 4 + i
                    mm = nc.tensor.transpose(
                        PSC[:, hb + i * 128:hb + (i + 1) * 128],
                        y_sb[:, dc * 128:(dc + 1) * 128], idf[:, :])
                CI(mm)
                CW(nc.vector)
                CI(nc.vector.tensor_copy(
                    yT_sb[:, g4 * 4:(g4 + 1) * 4, m * 128:(m + 1) * 128],
                    PSC[:, hb:hb + 512].rearrange("p (a q) -> p a q", a=4)))
                CI(nc.vector.tensor_copy(
                    yTf_sb[:, g4 * 4:(g4 + 1) * 4, :],
                    PSC[:, hb:hb + 512].rearrange("p (a q) -> p a q", a=4)))
                if g4 + 1 < 4:
                    CW(nc.tensor)
            CW(nc.tensor)
            for dc in range(16):
                mm = nc.tensor.matmul(PSD[:, 0:16], yTf_sb[:, dc, :],
                                      simn[:, dc, :],
                                      start=(dc == 0), stop=(dc == 15))
            CI(mm)

        def d1_tail(m=m):
            pe_d1(m)
            v = nc.vector
            CW(v)
            CI(v.tensor_scalar_mul(rt[:, 0:16], PSD[:, 0:16], rt[:, 159:160]))
            CW(v)
            CI(v.tensor_sub(rt[:, 0:16], rt[:, 0:16], gb_sb[:, :]))
            CW(v)
            CI(v.tensor_single_scalar(rt[:, 16:32], rt[:, 0:16], 0.0, ALU.is_gt))
            CW(v)
            CI(v.reduce_sum(rt[:, 32:33], rt[:, 16:32], AX.X))
            CI(v.reduce_max(rt[:, 33:34], rt[:, 0:16], AX.X))
            CW(v)
            CI(v.tensor_single_scalar(rt[:, 34:35], rt[:, 32:33], 0.0, ALU.is_equal))
            CI(v.tensor_scalar(rt[:, 48:64], rt[:, 0:16], rt[:, 33:34], None, ALU.is_ge))
            CW(v)
            CI(v.scalar_tensor_tensor(rt[:, 64:80], rt[:, 48:64], C(C_M1E9), rt[:, 0:16],
                                      ALU.mult, ALU.add))
            CW(v)
            CI(v.reduce_max(rt[:, 35:36], rt[:, 64:80], AX.X))
            CW(v)
            CI(v.tensor_scalar(rt[:, 80:96], rt[:, 0:16], rt[:, 35:36], None, ALU.is_ge))
            CW(v)
            CI(v.tensor_scalar_mul(rt[:, 80:96], rt[:, 80:96], rt[:, 34:35]))
            CW(v)
            CI(v.tensor_max(rt[:, 16:32], rt[:, 16:32], rt[:, 80:96]))
            CI(v.tensor_scalar_max(rt[:, 96:112], rt[:, 0:16], 0.0))
            CW(v)
            CI(v.tensor_scalar(rt[:, 96:112], rt[:, 96:112], C(C_1E9), None, ALU.add))
            CW(v)
            CI(v.tensor_mul(rt[:, 96:112], rt[:, 96:112], rt[:, 16:32]))
            CW(v)
            CI(v.tensor_scalar(rt[:, 96:112], rt[:, 96:112], C(C_M1E9), None, ALU.add))
            CW(nc.scalar)
            CI(nc.scalar.activation(rt[:, 112:128], rt[:, 96:112], AF.Exp))
            CW(v)
            CI(v.reduce_sum(rt[:, 36:37], rt[:, 112:128], AX.X))
            CW(v)
            CI(v.reciprocal(rt[:, 37:38], rt[:, 36:37]))
            CW(v)
            CI(v.tensor_scalar_mul(w_sb[:, m, :], rt[:, 112:128], rt[:, 37:38]))

        if m + 1 < MY:
            d1_pending = d1_tail
        else:
            d1_tail()
    bar()
    sWp = Sem("sWp")
    sTp = Sem("sTp")
    sVp = Sem("sVp")
    sDW = [Sem("sDW0"), Sem("sDW1")]
    for ip, (wsrc, acc) in enumerate([(qall, acc_q), (kall, acc_k), (vall, acc_v)]):
        for pn in range(4):
            pg = ip * 4 + pn
            for hh in range(2):
                ii = pg * 2 + hh
                W(nc.sync, sWp, ii)
                if ii >= 1:
                    W(nc.sync, sDW[(ii - 1) % 2], 16 * ((ii - 1) // 2 + 1))
                sDW[ii % 2].inc(nc.sync.dma_start(
                    out=stg_big[:, :, :],
                    in_=wsrc[:, pn, hh * 8:(hh + 1) * 8, :]), 16)
                W(nc.scalar, sDW[ii % 2], 16 * (ii // 2 + 1))
                if pg >= 2:
                    W(nc.scalar, sTp, MY * (pg - 1))
                sWp.inc(nc.scalar.copy(
                    wp_bf.ap().rearrange("p s a b -> p s (a b)")
                    [:, pg % 2, hh * 4096:(hh + 1) * 4096],
                    stg_big.ap().rearrange("p a b -> p (a b)")))
            for m in range(MY):
                tm = pg * MY + m
                W(nc.tensor, sWp, 2 * (pg + 1))
                W(nc.tensor, sVp, tm - 1)
                PS = PSD if tm % 2 == 0 else PSP
                mmi = None
                for dc in range(16):
                    mmi = nc.tensor.matmul(
                        PS[:, 0:512], yT_sb[:, dc, m * 128:(m + 1) * 128],
                        wp_bf[:, pg % 2, dc, :], start=(dc == 0), stop=(dc == 15))
                sTp.inc(mmi)
                W(nc.vector, sTp, tm + 1)
                for el in range(4):
                    e = pn * 4 + el
                    if pn == 0 and el == 0:
                        iv = nc.vector.tensor_scalar_mul(
                            acc[:, m, :], PS[:, el * 128:(el + 1) * 128],
                            w_sb[:, m, e:e + 1])
                    else:
                        iv = nc.vector.scalar_tensor_tensor(
                            acc[:, m, :], PS[:, el * 128:(el + 1) * 128],
                            w_sb[:, m, e:e + 1], acc[:, m, :], ALU.mult, ALU.add)
                    if el < 3:
                        VS(iv)
                sVp.inc(iv)
    bar()
    for m in range(MY):
        hb = (m % 2) * 512
        nc.tensor.transpose(PSC[:, hb:hb + 128], acc_q[:, m, :], idf[:, :])
        nc.tensor.transpose(PSC[:, hb + 128:hb + 256], acc_k[:, m, :], idf[:, :])
        bar()
        nc.vector.tensor_copy(qTf[:, m, :], PSC[:, hb:hb + 128])
        nc.vector.tensor_copy(ot_sb[:, 0, 0, :], PSC[:, hb + 128:hb + 256])
        nc.vector.tensor_copy(ot_sb[:, 0, 1, :], acc_v[:, m, :])
        bar()
        gdma_b(cc2_in[0, :, m * 128:(m + 1) * 128], ot_sb[:, 0, 0, :])
        gdma_b(cc2_in[1, :, m * 128:(m + 1) * 128], ot_sb[:, 0, 1, :])
        gdma_b_wait()
        bar()
    stD.close()
    bar()
    sCC.inc(nc.gpsimd.collective_compute(
        "AllGather", ALU.bypass, replica_groups=[list(range(8))],
        ins=[cc2_in.ap().opt()], outs=[cc2_out.ap().opt()]))
    nc.gpsimd.wait_ge(sCC.h, 2)
    bar()
    stF = ExitStack()
    kTa = stF.enter_context(nc.sbuf_tensor("kTa", [128, 8, MY * 128], BF16))
    va = stF.enter_context(nc.sbuf_tensor("va", [128, 8, MY * 128], BF16))
    fm_sb = stF.enter_context(nc.sbuf_tensor("fm_sb", [128, 1024], BF16))
    pf_sb = stF.enter_context(nc.sbuf_tensor("pf_sb", [128, 2, 512], BF16))
    rr2 = stF.enter_context(nc.sbuf_tensor("rr2", [1, 128], F32))
    rbc2 = stF.enter_context(nc.sbuf_tensor("rbc2", [128, 128], F32))
    aoTn = stF.enter_context(nc.sbuf_tensor("aoTn", [128, 128], F32))
    wT_sb = stF.enter_context(nc.sbuf_tensor("wT_sb", [16, 128], F32))
    wTf = stF.enter_context(nc.sbuf_tensor("wTf", [1, 2048], BF16))
    onesfb = stF.enter_context(nc.sbuf_tensor("onesfb", [1, 128], BF16))
    Zt = stF.enter_context(nc.sbuf_tensor("Zt", [128, G, 16, 128], BF16))
    op_bf = stF.enter_context(nc.sbuf_tensor("op_bf", [128, 2, 16, 512], BF16))
    z2 = stF.enter_context(nc.sbuf_tensor("z2", [128, G, D], F32))
    y2p = stF.enter_context(nc.sbuf_tensor("y2p", [128, 2, 512], F32))
    stF_big = stF.enter_context(nc.sbuf_tensor("stF_big", [128, 4, 512], F32))
    rt2 = stF.enter_context(nc.sbuf_tensor("rt2", [128, 32], F32))

    for r in range(8):
        dma_b(kTa[:, r, :], cc2_out[r, 0, :, :])
        dma_b(va[:, r, :], cc2_out[r, 1, :, :])
    dma_b(fm_sb[:, :], fmask[:, :])
    dma_b_wait()
    bar()
    nc.vector.memset(onesfb[:, :], 1.0)
    bar()

    sT2 = Sem("sT2")
    sE2 = Sem("sE2")
    sM2 = Sem("sM2")
    sTt2 = Sem("sTt2")
    sOW = Sem("sOW")
    sTO = Sem("sTO")
    sVO = Sem("sVO")
    sFW = [Sem("sFW0"), Sem("sFW1")]
    sY2 = [Sem("sY20"), Sem("sY21")]

    GBC = 0
    ztval = {}
    for hg in range(NG):
        tiles = list(range(hg * G, (hg + 1) * G))
        for mi, m in enumerate(tiles):
            j = m
            ngr = 2 * (j + 1)
            m2count = {}

            def emit_pv2(gg, ngr=ngr):
                first_g = (gg == 0)
                last_g = (gg == ngr - 1)
                if gg >= ngr - 2:
                    W(nc.tensor, sM2, m2count[gg])
                else:
                    W(nc.tensor, sE2, GBC + gg + 1)
                if first_g:
                    W(nc.tensor, sVO, sVO.n)
                for i in range(4):
                    a = gg * 4 + i
                    first = (a == 0)
                    last = (a == ngr * 4 - 1)
                    nc.tensor.matmul(
                        PSB[:, 0:128],
                        va[:, a % 8, (a // 8) * 128:(a // 8 + 1) * 128],
                        pf_sb[:, gg % 2, i * 128:(i + 1) * 128],
                        start=first, stop=last)
                    mm = nc.tensor.matmul(PSP[0:1, 0:128], onesb[:, :],
                                          pf_sb[:, gg % 2, i * 128:(i + 1) * 128],
                                          start=first, stop=last)
                    if last:
                        sTt2.inc(mm)

            for g in range(ngr):
                gi = GBC + g
                W(nc.tensor, sE2, gi - 1)
                ha = (g % 2) * 512
                mm = None
                for i in range(4):
                    a = g * 4 + i
                    mm = nc.tensor.matmul(
                        PSA[:, ha + i * 128:ha + (i + 1) * 128],
                        kTa[:, a % 8, (a // 8) * 128:(a // 8 + 1) * 128], qTf[:, m, :],
                        start=True, stop=True)
                sT2.inc(mm)
                if g >= 1:
                    emit_pv2(g - 1)
                W(nc.scalar, sT2, gi + 1)
                sE2.inc(nc.scalar.activation(
                    pf_sb[:, g % 2, :], PSA[:, ha:ha + 512], AF.Exp,
                    bias=C(C_BF), scale=C(C_SF)))
                if g >= ngr - 2:
                    bi0 = (g - (ngr - 2)) * 4
                    W(nc.vector, sE2, gi + 1)
                    sM2.inc(nc.vector.tensor_mul(
                        pf_sb[:, g % 2, :], pf_sb[:, g % 2, :],
                        fm_sb[:, bi0 * 128:(bi0 + 4) * 128]))
                    m2count[g] = sM2.n
            emit_pv2(ngr - 1)
            GBC += ngr
            W(nc.vector, sTt2, sTt2.n)
            CI(nc.vector.tensor_copy(rr2[0:1, :], PSP[0:1, 0:128]))
            CW(nc.vector)
            CI(nc.vector.reciprocal(rr2[:, :], rr2[:, :]))
            CW(nc.tensor)
            CI(nc.tensor.matmul(PSC[:, 0:128], onesf[:, :], rr2[0:1, :],
                                start=True, stop=True))
            CI(nc.tensor.transpose(PSC[0:16, 128:256], w_sb[:, m, :], idf[:, :]))
            CW(nc.vector)
            CI(nc.vector.tensor_copy(rbc2[:, :], PSC[:, 0:128]))
            W(nc.vector, sG, sG.n)
            CI(nc.vector.tensor_copy(wT_sb[:, :], PSC[0:16, 128:256]))
            CW(nc.gpsimd)
            gdma_b(wTf[0:1, :], wT_sb[:, :])
            CW(nc.vector)
            CI(nc.vector.tensor_mul(aoTn[:, :], PSB[:, 0:128], rbc2[:, :]))
            pvfree = sCH.n
            W(nc.tensor, sG, sG.n)
            CW(nc.tensor)
            for eg in range(4):
                for el in range(4):
                    e = eg * 4 + el
                    mm = nc.tensor.matmul(
                        PSC[:, 512 + el * 128:512 + (el + 1) * 128],
                        onesfb[:, :], wTf[0:1, e * 128:(e + 1) * 128],
                        start=True, stop=True)
                CI(mm)
                CW(nc.vector)
                CI(nc.vector.tensor_tensor(
                    Zt[:, mi, eg * 4:(eg + 1) * 4, :],
                    aoTn.ap().unsqueeze(1).broadcast_to([128, 4, 128]),
                    PSC[:, 512:1024].rearrange("p (a q) -> p a q", a=4),
                    ALU.mult))
                if eg + 1 < 4:
                    CW(nc.tensor)
            ztval[mi] = sCH.n
        for pn in range(4):
            pp = hg * 4 + pn
            for hh in range(4):
                ii = pp * 4 + hh
                W(nc.sync, sOW, ii)
                if ii >= 1:
                    W(nc.sync, sFW[(ii - 1) % 2], 16 * ((ii - 1) // 2 + 1))
                sFW[ii % 2].inc(nc.sync.dma_start(
                    out=stF_big[:, :, :], in_=oall[:, pn, hh * 4:(hh + 1) * 4, :]), 16)
                W(nc.scalar, sFW[ii % 2], 16 * (ii // 2 + 1))
                if pp >= 2:
                    W(nc.scalar, sTO, G * (pp - 1))
                sOW.inc(nc.scalar.copy(
                    op_bf.ap().rearrange("p s a b -> p s (a b)")
                    [:, pp % 2, hh * 2048:(hh + 1) * 2048],
                    stF_big.ap().rearrange("p a b -> p (a b)")))
            for mi, m in enumerate(tiles):
                tm2 = pp * G + mi
                W(nc.sync, sVO, tm2 - 1)
                sY2[tm2 % 2].inc(nc.sync.dma_start(
                    out=y2p[:, tm2 % 2, :],
                    in_=y_hbm[m, :, pn * 512:(pn + 1) * 512]), 16)
                W(nc.tensor, sOW, 4 * (pp + 1))
                W(nc.tensor, sVO, tm2 - 1)
                if pn == 0:
                    W(nc.tensor, sCH, ztval[mi])
                PS = PSD if tm2 % 2 == 0 else PSP
                mo = None
                for e in range(16):
                    mo = nc.tensor.matmul(
                        PS[:, 0:512], Zt[:, mi, e, :], op_bf[:, pp % 2, e, :],
                        start=(e == 0), stop=(e == 15))
                sTO.inc(mo)
                W(nc.vector, sTO, tm2 + 1)
                W(nc.vector, sY2[tm2 % 2], 16 * (tm2 // 2 + 1))
                if pn == 0 and mi == 0:
                    W(nc.vector, sG, sG.n)
                sVO.inc(nc.vector.tensor_add(
                    z2[:, mi, pn * 512:(pn + 1) * 512], PS[:, 0:512],
                    y2p[:, tm2 % 2, :]))
        W(nc.vector, sVO, 4 * G * (hg + 1))
        for mi, m in enumerate(tiles):
            for cch in range(4):
                CI(nc.vector.bn_stats(rt2[:, cch * 6:6 + cch * 6],
                                      z2[:, mi, cch * 512:(cch + 1) * 512]))
            CW(nc.vector)
            CI(nc.vector.bn_aggr(rt2[:, 24:26], rt2[:, 0:24]))
            CW(nc.scalar)
            CI(nc.scalar.activation(rt2[:, 26:27], rt2[:, 25:26], AF.Sqrt,
                                    bias=C(C_EPS5)))
            CW(nc.vector)
            CI(nc.vector.reciprocal(rt2[:, 27:28], rt2[:, 26:27]))
            CW(nc.vector)
            CI(nc.vector.tensor_mul(rt2[:, 28:29], rt2[:, 24:25], rt2[:, 27:28]))
            CW(nc.vector)
            CI(nc.vector.tensor_scalar_mul(rt2[:, 28:29], rt2[:, 28:29], C(C_M1)))
            CW(nc.vector)
            CI(nc.vector.tensor_scalar(z2[:, mi, :], z2[:, mi, :], rt2[:, 27:28],
                                       rt2[:, 28:29], ALU.mult, ALU.add))
            CW(nc.gpsimd)
            gdma_b(out_rows[m, :, :], z2[:, mi, :])
    gdma_b_wait()
    stF.close()
    bar()
    st_base.close()
    return nc


def make_in_maps(inp, S):
    MY = (S // 128) // 8
    x = np.ascontiguousarray(np.asarray(inp["L0_attn_x"], dtype=np.float32).reshape(S, D))
    wq = np.asarray(inp["L0_attn_wq"], dtype=np.float32)
    wk = np.asarray(inp["L0_attn_wk"], dtype=np.float32)
    wv = np.asarray(inp["L0_attn_wv"], dtype=np.float32)
    wo = np.asarray(inp["L0_attn_wo"], dtype=np.float32)
    sim = np.asarray(inp["L0_ffn_sim_matrix"], dtype=np.float32)
    gates = np.asarray(inp["L0_ffn_gates"], dtype=np.float32)
    qp = np.asarray(inp["L0_ffn_q_proj"], dtype=np.float32)
    kp = np.asarray(inp["L0_ffn_k_proj"], dtype=np.float32)
    vp = np.asarray(inp["L0_ffn_v_proj"], dtype=np.float32)
    op = np.asarray(inp["L0_ffn_o_proj"], dtype=np.float32)

    xT = np.ascontiguousarray(x.T).reshape(16, 128, S)
    woT = np.ascontiguousarray(wo.T).reshape(16, 128, D)
    simm = np.ascontiguousarray(sim).reshape(16, 128, 16)
    def _pm(w):
        return np.ascontiguousarray(
            w.reshape(16, 128, 4, 512).transpose(1, 2, 0, 3))
    qall = _pm(qp.transpose(1, 0, 2).reshape(D, D))
    kall = _pm(kp.transpose(1, 0, 2).reshape(D, D))
    vall = _pm(vp.transpose(1, 0, 2).reshape(D, D))
    oall = _pm(op.reshape(D, D))
    lvv = np.ascontiguousarray(np.concatenate([
        np.asarray(inp["L0_attn_l_q1"], dtype=np.float32),
        np.asarray(inp["L0_attn_l_k1"], dtype=np.float32),
        np.asarray(inp["L0_attn_l_q2"], dtype=np.float32),
        np.asarray(inp["L0_attn_l_k2"], dtype=np.float32)]).reshape(1, 32))
    gatesr = np.ascontiguousarray(gates.reshape(1, 16))

    in_maps = []
    for c in range(8):
        h, half = c // 2, c % 2
        r1 = h * 512 + half * 256
        r2 = (h + 4) * 512 + half * 256
        wqk_c = np.concatenate([
            wq[r1:r1 + 256].T, wq[r2:r2 + 256].T,
            wk[r1:r1 + 256].T, wk[r2:r2 + 256].T], axis=1)
        wqk_c = np.ascontiguousarray(wqk_c).reshape(16, 128, 1024)
        wv_c = np.ascontiguousarray(wv[c * 256:(c + 1) * 256].T).reshape(16, 128, 256)
        x_rows_c = np.ascontiguousarray(np.stack(
            [x[(8 * j + c) * 128:(8 * j + c + 1) * 128] for j in range(MY)]))
        p = np.arange(128)[:, None]
        q = np.arange(128)[None, :]
        fm = np.zeros((128, 8, 128), dtype=np.float32)
        for bi in range(8):
            fm[:, bi, :] = ((bi - c) * 128 + p <= q)
        fmask_c = np.ascontiguousarray(fm.reshape(128, 1024)).astype(ml_dtypes.bfloat16)
        in_maps.append({
            "xT": xT, "x_rows": x_rows_c, "wqk": wqk_c, "wv": wv_c, "lv": lvv,
            "woT": woT, "simm": simm, "gates": gatesr, "qall": qall,
            "kall": kall, "vall": vall, "oall": oall, "fmask": fmask_c,
        })
    return in_maps


def assemble(results, S):
    MY = (S // 128) // 8
    out = np.empty((S, D), dtype=np.float32)
    for c in range(8):
        o = np.asarray(results[c]["out_rows"], dtype=np.float32).reshape(MY, 128, D)
        for j in range(MY):
            out[(8 * j + c) * 128:(8 * j + c + 1) * 128] = o[j]
    return out.reshape(1, S, D)


def kernel(**inp):
    S = 8192
    nc = build(S)
    in_maps = make_in_maps(inp, S)
    res = run_bass_kernel_spmd(nc, in_maps, core_ids=list(range(8)))
    return assemble(res.results, S)


# version 14
# speedup vs baseline: 1.0192x; 1.0192x over previous
import math
from contextlib import ExitStack

import numpy as np
import ml_dtypes
import concourse.bass as bass
import concourse.mybir as mybir
from concourse.bass_utils import run_bass_kernel_spmd
from concourse.aot_env import allow_ici_all_to_all

F32 = mybir.dt.float32
BF16 = mybir.dt.bfloat16
AF = mybir.ActivationFunctionType
ALU = mybir.AluOpType
AX = mybir.AxisListType

D = 2048
L_INIT = 0.8 - 0.6 * math.exp(-0.3 * 1)
C_SCALE = 1.0 - L_INIT

C_S16, C_BA, C_SF, C_BF, C_EPS6, C_I32, C_EPS5, C_CS, C_SQ2048, C_1E9, \
    C_M1E9, C_M1, C_EPS12, C_LI = range(14)
NCONST = 14


def build(S):
    NT = S // 128
    TC = S // 1024
    QT = S // 512
    MY = NT // 8
    TPC = S // 8
    G = 4 if MY % 4 == 0 else 1
    NG = MY // G

    nc = bass.Bass("TRN2", target_bir_lowering=False)

    xT = nc.dram_tensor("xT", [16, 128, S], F32, kind="ExternalInput")
    x_rows = nc.dram_tensor("x_rows", [MY, 128, D], F32, kind="ExternalInput")
    wqk = nc.dram_tensor("wqk", [16, 128, 1024], F32, kind="ExternalInput")
    wv = nc.dram_tensor("wv", [16, 128, 256], F32, kind="ExternalInput")
    lv = nc.dram_tensor("lv", [1, 32], F32, kind="ExternalInput")
    woT = nc.dram_tensor("woT", [16, 128, D], F32, kind="ExternalInput")
    simm = nc.dram_tensor("simm", [16, 128, 16], F32, kind="ExternalInput")
    gates = nc.dram_tensor("gates", [1, 16], F32, kind="ExternalInput")
    qall = nc.dram_tensor("qall", [128, 4, 16, 512], F32, kind="ExternalInput")
    kall = nc.dram_tensor("kall", [128, 4, 16, 512], F32, kind="ExternalInput")
    vall = nc.dram_tensor("vall", [128, 4, 16, 512], F32, kind="ExternalInput")
    oall = nc.dram_tensor("oall", [128, 4, 16, 512], F32, kind="ExternalInput")
    fmask = nc.dram_tensor("fmask", [128, 1024], BF16, kind="ExternalInput")

    out_rows = nc.dram_tensor("out_rows", [MY, 128, D], F32, kind="ExternalOutput")

    qt_hbm = nc.dram_tensor("qt_hbm", [4, 128, S], BF16)
    kt_hbm = nc.dram_tensor("kt_hbm", [4, 128, S], BF16)
    v_hbm = nc.dram_tensor("v_hbm", [NT, 128, 256], BF16)
    y_hbm = nc.dram_tensor("y_hbm", [MY, 128, D], F32)
    cc_in = nc.dram_tensor("cc_in", [8, 256, TPC], BF16)
    cc_out = nc.dram_tensor("cc_out", [8, 256, TPC], BF16)
    cc2_in = nc.dram_tensor("cc2_in", [2, 128, MY * 128], BF16)
    cc2_out = nc.dram_tensor("cc2_out", [8, 2, 128, MY * 128], BF16)

    cvals = np.zeros((128, NCONST), dtype=np.float32)
    cvals[:, C_S16] = 1.0 / 16.0
    cvals[:, C_BA] = -4.0
    cvals[:, C_SF] = 1.0 / math.sqrt(128.0)
    cvals[:, C_BF] = -2.0
    cvals[:, C_EPS6] = 1e-6
    cvals[:, C_I32] = 1.0 / 32.0
    cvals[:, C_EPS5] = 1e-5
    cvals[:, C_CS] = C_SCALE
    cvals[:, C_SQ2048] = math.sqrt(2048.0)
    cvals[:, C_1E9] = 1e9
    cvals[:, C_M1E9] = -1e9
    cvals[:, C_M1] = -1.0
    cvals[:, C_EPS12] = 1e-12
    cvals[:, C_LI] = L_INIT
    cst_d = nc.inline_tensor(cvals, "cst_d")
    ident_d = nc.inline_tensor(np.eye(128, dtype=np.float32), "ident_d")
    dmv = np.zeros((128, 896), dtype=np.float32)
    for p in range(128):
        dmv[p, 384 + p:896] = 1.0
    dm_d = nc.inline_tensor(dmv, "dm_d")

    class Sem:
        def __init__(self, name):
            self.h = nc.alloc_semaphore(name)
            self.n = 0

        def inc(self, inst, k=1):
            inst.then_inc(self.h, k)
            self.n += k
            return self.n

    def W(eng, s, v):
        if v > 0:
            eng.wait_ge(s.h, v)

    def bar():
        nc.all_engine_barrier()

    sQ = Sem("sQ")
    sG = Sem("sG")

    def dma_b(out_ap, in_ap):
        sQ.inc(nc.sync.dma_start(out=out_ap, in_=in_ap), 16)

    def dma_b_wait():
        nc.sync.wait_ge(sQ.h, sQ.n)

    def gdma_b(out_ap, in_ap):
        sG.inc(nc.gpsimd.dma_start(out=out_ap, in_=in_ap), 16)

    def gdma_b_wait():
        nc.gpsimd.wait_ge(sG.h, sG.n)

    vch = Sem("vch")
    sCH = Sem("sCH")

    def CI(inst):
        sCH.inc(inst)
        return inst

    def CW(eng):
        eng.wait_ge(sCH.h, sCH.n)

    def VS(inst):
        vch.inc(inst)
        nc.vector.wait_ge(vch.h, vch.n)

    st_base = ExitStack()
    cst = st_base.enter_context(nc.sbuf_tensor("cst", [128, NCONST], F32))
    idf = st_base.enter_context(nc.sbuf_tensor("idf", [128, 128], F32))
    onesf = st_base.enter_context(nc.sbuf_tensor("onesf", [1, 128], F32))
    onesc = st_base.enter_context(nc.sbuf_tensor("onesc", [128, 1], F32))
    onesb = st_base.enter_context(nc.sbuf_tensor("onesb", [128, 1], BF16))
    lam = st_base.enter_context(nc.sbuf_tensor("lam", [128, 1], F32))
    lamn = st_base.enter_context(nc.sbuf_tensor("lamn", [128, 1], F32))
    w_sb = st_base.enter_context(nc.sbuf_tensor("w_sb", [128, MY, 16], F32))
    qTf = st_base.enter_context(nc.sbuf_tensor("qTf", [128, MY, 128], BF16))
    smallf = st_base.enter_context(nc.sbuf_tensor("smallf", [128, 56], F32))
    woT_bf = st_base.enter_context(nc.sbuf_tensor("woT_bf", [128, 16, D], BF16))
    PSA = st_base.enter_context(nc.psum_tensor("PSA", [128, 1024], F32))
    PSB = st_base.enter_context(nc.psum_tensor("PSB", [128, 1024], F32))
    PSC = st_base.enter_context(nc.psum_tensor("PSC", [128, 1024], F32))
    PSD = st_base.enter_context(nc.psum_tensor("PSD", [128, 512], F32))
    PSP = st_base.enter_context(nc.psum_tensor("PSP", [128, 512], F32))

    C = lambda i: cst[:, i:i + 1]
    C0 = lambda i: cst[0:1, i:i + 1]

    dma_b(cst[:, :], cst_d[:, :])
    dma_b(idf[:, :], ident_d[:, :])
    dma_b(smallf[0:1, 0:32], lv[:, :])
    dma_b_wait()
    bar()
    nc.vector.memset(onesf[:, :], 1.0)
    nc.vector.memset(onesc[:, :], 1.0)
    nc.vector.memset(onesb[:, :], 1.0)
    nc.vector.tensor_mul(smallf[0:1, 32:40], smallf[0:1, 0:8], smallf[0:1, 8:16])
    nc.vector.tensor_mul(smallf[0:1, 40:48], smallf[0:1, 16:24], smallf[0:1, 24:32])
    bar()
    nc.vector.reduce_sum(smallf[0:1, 48:49], smallf[0:1, 32:40], AX.X)
    nc.vector.reduce_sum(smallf[0:1, 49:50], smallf[0:1, 40:48], AX.X)
    bar()
    nc.scalar.activation(smallf[0:1, 50:52], smallf[0:1, 48:50], AF.Exp)
    bar()
    nc.vector.tensor_sub(smallf[0:1, 52:53], smallf[0:1, 50:51], smallf[0:1, 51:52])
    bar()
    nc.vector.tensor_scalar_add(smallf[0:1, 53:54], smallf[0:1, 52:53], C0(C_LI))
    bar()
    nc.tensor.matmul(PSD[:, 0:1], onesf[:, :], smallf[0:1, 53:54], start=True, stop=True)
    bar()
    nc.vector.tensor_copy(lam[:, :], PSD[:, 0:1])
    bar()
    nc.vector.tensor_scalar_mul(lamn[:, :], lam[:, :], C(C_M1))
    bar()
    stA = ExitStack()
    wqk_bf = stA.enter_context(nc.sbuf_tensor("wqk_bf", [128, 16, 1024], BF16))
    wv_bf = stA.enter_context(nc.sbuf_tensor("wv_bf", [128, 16, 256], BF16))

    with nc.sbuf_tensor("wtmp", [128, 16, 1280], F32) as wtmp:
        for dc in range(16):
            dma_b(wtmp[:, dc, 0:1024], wqk[dc, :, :])
            dma_b(wtmp[:, dc, 1024:1280], wv[dc, :, :])
        dma_b_wait()
        bar()
        for dc in range(16):
            nc.scalar.copy(wqk_bf[:, dc, :], wtmp[:, dc, 0:1024])
            nc.scalar.copy(wv_bf[:, dc, :], wtmp[:, dc, 1024:1280])
        bar()
    xbf = stA.enter_context(nc.sbuf_tensor("xbf", [128, 2, 16, 1024], BF16))
    xst = stA.enter_context(nc.sbuf_tensor("xst", [128, 2, 1024], F32))
    kq_st = stA.enter_context(nc.sbuf_tensor("kq_st", [128, 2, 1024], BF16))
    v_st = stA.enter_context(nc.sbuf_tensor("v_st", [128, 2, 256], BF16))

    sXd = [Sem("sXd0"), Sem("sXd1")]
    sXc = Sem("sXc")
    sTq = Sem("sTq")
    sTv = Sem("sTv")
    sVq = Sem("sVq")
    sVv = Sem("sVv")
    sGq = [Sem("sGq0"), Sem("sGq1")]
    sGv = [Sem("sGv0"), Sem("sGv1")]

    for tc in range(TC):
        for dc in range(16):
            i = tc * 16 + dc
            W(nc.sync, sXc, i - 1)
            sXd[i % 2].inc(nc.sync.dma_start(
                out=xst[:, i % 2, :], in_=xT[dc, :, tc * 1024:(tc + 1) * 1024]), 16)
            W(nc.scalar, sXd[i % 2], 16 * (i // 2 + 1))
            if tc >= 2:
                W(nc.scalar, sTq, 8 * (tc - 1))
                W(nc.scalar, sTv, 8 * (tc - 1))
            sXc.inc(nc.scalar.copy(xbf[:, tc % 2, dc, :], xst[:, i % 2, :]))
        for qc in range(8):
            gg = tc * 8 + qc
            W(nc.tensor, sXc, 16 * (tc + 1))
            W(nc.tensor, sVq, gg - 1)
            PS = PSA if gg % 2 == 0 else PSB
            mm = None
            for th in range(2):
                for dc in range(16):
                    mm = nc.tensor.matmul(
                        PS[:, th * 512:(th + 1) * 512],
                        wqk_bf[:, dc, qc * 128:(qc + 1) * 128],
                        xbf[:, tc % 2, dc, th * 512:(th + 1) * 512],
                        start=(dc == 0), stop=(dc == 15))
            sTq.inc(mm)
            W(nc.vector, sTq, gg + 1)
            W(nc.vector, sGq[gg % 2], 16 * (gg // 2))
            sVq.inc(nc.vector.tensor_copy(kq_st[:, gg % 2, :], PS[:, :]))
            W(nc.gpsimd, sVq, gg + 1)
            dst = qt_hbm if qc < 4 else kt_hbm
            sGq[gg % 2].inc(nc.gpsimd.dma_start(
                out=dst[qc % 4, :, tc * 1024:(tc + 1) * 1024],
                in_=kq_st[:, gg % 2, :]), 16)
        for vt in range(8):
            gv = tc * 8 + vt
            W(nc.tensor, sXc, 16 * (tc + 1))
            W(nc.tensor, sVv, gv - 1)
            half = (gv % 2) * 512
            mm = None
            for dc in range(16):
                mm = nc.tensor.matmul(
                    PSC[:, half:half + 256],
                    xbf[:, tc % 2, dc, vt * 128:(vt + 1) * 128],
                    wv_bf[:, dc, :],
                    start=(dc == 0), stop=(dc == 15))
            sTv.inc(mm)
            W(nc.vector, sTv, gv + 1)
            W(nc.vector, sGv[gv % 2], 16 * (gv // 2))
            sVv.inc(nc.vector.tensor_copy(v_st[:, gv % 2, :], PSC[:, half:half + 256]))
            W(nc.gpsimd, sVv, gv + 1)
            sGv[gv % 2].inc(nc.gpsimd.dma_start(
                out=v_hbm[tc * 8 + vt, :, :], in_=v_st[:, gv % 2, :]), 16)
    for s_ in sGq + sGv:
        nc.gpsimd.wait_ge(s_.h, s_.n)
    stA.close()
    bar()
    stB = ExitStack()
    KT = stB.enter_context(nc.sbuf_tensor("KT", [128, 4, S], BF16))
    Vt = stB.enter_context(nc.sbuf_tensor("Vt", [128, NT, 256], BF16))
    qt_sb = stB.enter_context(nc.sbuf_tensor("qt_sb", [128, 1, 4, 512], BF16))
    pt_sb = stB.enter_context(nc.sbuf_tensor("pt_sb", [128, 7, 1024], BF16))
    dm_sb = stB.enter_context(nc.sbuf_tensor("dm_sb", [128, 896], BF16))
    wstg = stB.enter_context(nc.sbuf_tensor("wstg", [128, 2, 512], F32))
    rpack = stB.enter_context(nc.sbuf_tensor("rpack", [1, 1024], F32))
    rrsb = rpack[0:1, :]
    rbc_sb = stB.enter_context(nc.sbuf_tensor("rbc_sb", [128, 1024], F32))

    oca = stB.enter_context(nc.sbuf_tensor("oca", [128, 512], F32))
    ocb = stB.enter_context(nc.sbuf_tensor("ocb", [128, 512], F32))
    sq4_sb = stB.enter_context(nc.sbuf_tensor("sq4_sb", [128, 1024], F32))
    obt4 = sq4_sb
    gs_sb = stB.enter_context(nc.sbuf_tensor("gs_sb", [128, 32], F32))
    sqr_sb = stB.enter_context(nc.sbuf_tensor("sqr_sb", [128, 32], F32))
    rs_sb = stB.enter_context(nc.sbuf_tensor("rs_sb", [128, 32], F32))
    obf4 = stB.enter_context(nc.sbuf_tensor("obf4", [128, 1024], BF16))
    otmp = obf4.ap().bitcast(F32)

    pass
    for ch in range(4):
        dma_b(KT[:, ch, :], kt_hbm[ch, :, :])
    for n in range(NT):
        dma_b(Vt[:, n, :], v_hbm[n, :, :])
    dma_b_wait()
    bar()
    for hh in range(2):
        n0, n1 = hh * 448, (hh + 1) * 448
        dma_b(wstg.ap().rearrange("p a b -> p (a b)")[:, 0:448], dm_d[:, n0:n1])
        dma_b_wait()
        bar()
        nc.vector.tensor_copy(dm_sb[:, n0:n1],
                              wstg.ap().rearrange("p a b -> p (a b)")[:, 0:448])
        bar()

    sQt = [Sem("sQt0"), Sem("sQt1")]
    sTS = Sem("sTS")
    sEx = Sem("sEx")
    sMk = Sem("sMk")
    sTt = Sem("sTt")
    sVr = Sem("sVr")
    sTr = Sem("sTr")
    sVb = Sem("sVb")
    sVc = Sem("sVc")
    sTf = Sem("sTf")
    sSq = Sem("sSq")
    sVg = Sem("sVg")
    sSr = Sem("sSr")
    sVo = Sem("sVo")
    sTb = Sem("sTb")
    sVf = Sem("sVf")
    sGo = Sem("sGo")
    sTP = Sem("sTP")

    sWoD = [Sem("sWoD0"), Sem("sWoD1")]
    sWoC = Sem("sWoC")

    def emit_wot(i):
        dc, h = i // 4, i % 4
        W(nc.sync, sWoC, i - 1)
        sWoD[i % 2].inc(nc.sync.dma_start(
            out=wstg[:, i % 2, :], in_=woT[dc, :, h * 512:(h + 1) * 512]), 16)
        W(nc.scalar, sWoD[i % 2], 16 * (i // 2 + 1))
        sWoC.inc(nc.scalar.copy(woT_bf[:, dc, h * 512:(h + 1) * 512],
                                wstg[:, i % 2, :]))

    KB = 0
    DD = 5
    NSLOT = 7
    mkcount = {}
    epi = {}
    for t in range(QT):
        nk = 4 * t + 4
        if t >= 1:
            W(nc.sync, sTS, 4 * t * (t + 1))
        for ch in range(4):
            sQt[0].inc(nc.sync.dma_start(
                out=qt_sb[:, 0, ch, :],
                in_=qt_hbm[ch, :, t * 512:(t + 1) * 512]), 16)
        for i in range(4 * t, min(4 * t + 4, 64)):
            emit_wot(i)

        def emit_pv(j, first, last, t=t, nk=nk):
            if j >= nk - 4:
                W(nc.tensor, sMk, mkcount[(t, j)])
            if j == 0:
                W(nc.tensor, sVo, t)
                W(nc.tensor, sVf, t)
                W(nc.tensor, sVb, t)
                W(nc.tensor, sVr, t)
            for dv in range(2):
                PO = PSB if dv == 0 else PSC
                for sm in range(2):
                    nc.tensor.matmul(
                        PO[:, sm * 512:(sm + 1) * 512],
                        Vt[:, j, dv * 128:(dv + 1) * 128],
                        pt_sb[:, j % NSLOT, sm * 512:(sm + 1) * 512],
                        start=first, stop=last)
            nc.tensor.matmul(PSD[0:1, 0:512], onesb[:, :],
                             pt_sb[:, j % NSLOT, 0:512], start=first, stop=last)
            mm = nc.tensor.matmul(PSP[0:1, 0:512], onesb[:, :],
                                  pt_sb[:, j % NSLOT, 512:1024], start=first, stop=last)
            sTP.inc(mm)

        for k in range(nk):
            g = KB + k
            W(nc.tensor, sEx, 2 * g)
            if k == 0:
                W(nc.tensor, sQt[0], 64 * (t + 1))
            for sm in range(2):
                for dh in range(2):
                    ch = sm * 2 + dh
                    mm = nc.tensor.matmul(
                        PSA[:, sm * 512:(sm + 1) * 512],
                        KT[:, ch, k * 128:(k + 1) * 128],
                        qt_sb[:, 0, ch, :],
                        start=(dh == 0), stop=(dh == 1))
                sTS.inc(mm)
            if k == 0 and "rbc" in epi:
                epi.pop("rbc")()
            if k == 1 and "fwd" in epi:
                epi.pop("fwd")()
            if k == 4 and "bwd" in epi:
                epi.pop("bwd")()
            if k >= DD:
                emit_pv(k - DD, first=(k - DD == 0), last=False)
            W(nc.scalar, sTP, KB + max(0, k - (NSLOT - 1)))
            for sm in range(2):
                W(nc.scalar, sTS, 2 * g + sm + 1)
                sEx.inc(nc.scalar.activation(
                    pt_sb[:, k % NSLOT, sm * 512:(sm + 1) * 512],
                    PSA[:, sm * 512:(sm + 1) * 512], AF.Exp,
                    bias=C(C_BA), scale=C(C_S16)))
            if k == 1 and "sq" in epi:
                epi.pop("sq")()
            if k == 2 and "sr" in epi:
                epi.pop("sr")()
            if k >= nk - 4:
                i = k - (nk - 4)
                W(nc.vector, sEx, 2 * g + 2)
                sMk.inc(nc.vector.tensor_mul(
                    pt_sb[:, k % NSLOT, :].rearrange("p (s q) -> p s q", s=2),
                    pt_sb[:, k % NSLOT, :].rearrange("p (s q) -> p s q", s=2),
                    dm_sb[:, 384 - 128 * i:896 - 128 * i]
                    .unsqueeze(1).broadcast_to([128, 2, 512])))
                mkcount[(t, k)] = sMk.n
        for nm in ("rbc", "fwd", "sq", "sr", "bwd"):
            if nm in epi:
                epi.pop(nm)()
        for j in range(max(0, nk - DD), nk):
            emit_pv(j, first=(j == 0), last=(j == nk - 1))
        KB += nk

        W(nc.vector, sTP, KB)
        CI(nc.vector.reciprocal(rrsb[:, 0:512], PSD[0:1, 0:512]))
        CI(nc.vector.reciprocal(rrsb[:, 512:1024], PSP[0:1, 0:512]))
        CW(nc.vector)
        sVr.inc(nc.vector.tensor_scalar_mul(
            rrsb[:, 512:1024], rrsb[:, 512:1024], lamn[0:1, :]))

        def pe_rbc(t=t):
            W(nc.tensor, sVr, t + 1)
            nc.tensor.matmul(PSD[:, 0:512], onesf[:, :], rrsb[:, 0:512],
                             start=True, stop=True)
            sTr.inc(nc.tensor.matmul(PSP[:, 0:512], onesf[:, :],
                                     rrsb[:, 512:1024], start=True, stop=True))

        W(nc.vector, sTr, t + 1)
        CI(nc.vector.tensor_copy(rbc_sb[:, 0:512], PSD[:, 0:512]))
        sVb.inc(nc.vector.tensor_copy(rbc_sb[:, 512:1024], PSP[:, 0:512]))
        W(nc.vector, sVb, t + 1)
        W(nc.vector, sGo, 128 * t)
        CI(nc.vector.tensor_mul(oca[:, :], PSB[:, 0:512], rbc_sb[:, 0:512]))
        CI(nc.vector.tensor_mul(otmp[:, :], PSB[:, 512:1024], rbc_sb[:, 512:1024]))
        CW(nc.vector)
        CI(nc.vector.tensor_add(oca[:, :], oca[:, :], otmp[:, :]))
        CI(nc.vector.tensor_mul(ocb[:, :], PSC[:, 0:512], rbc_sb[:, 0:512]))
        CW(nc.vector)
        CI(nc.vector.tensor_mul(otmp[:, :], PSC[:, 512:1024], rbc_sb[:, 512:1024]))
        CW(nc.vector)
        sVc.inc(nc.vector.tensor_add(ocb[:, :], ocb[:, :], otmp[:, :]))

        def pe_fwd(t=t):
            W(nc.tensor, sVc, t + 1)
            for qs in range(4):
                nc.tensor.transpose(PSB[:, qs * 256:qs * 256 + 128],
                                    oca[:, qs * 128:(qs + 1) * 128], idf[:, :])
                mm = nc.tensor.transpose(PSB[:, qs * 256 + 128:qs * 256 + 256],
                                         ocb[:, qs * 128:(qs + 1) * 128], idf[:, :])
            sTf.inc(mm)

        def sc_sq(t=t):
            W(nc.scalar, sTf, t + 1)
            sSq.inc(nc.scalar.activation(sq4_sb[:, :], PSB[:, :], AF.Square))

        W(nc.vector, sSq, t + 1)
        sVg.inc(nc.vector.reduce_sum(
            gs_sb[:, :], sq4_sb.ap().rearrange("p (g k) -> p g k", k=32), AX.X))
        def sc_sr(t=t):
            W(nc.scalar, sVg, t + 1)
            sSr.inc(nc.scalar.activation(sqr_sb[:, :], gs_sb[:, :], AF.Sqrt,
                                         bias=C(C_EPS6), scale=C(C_I32)))

        W(nc.vector, sSr, t + 1)
        VS(nc.vector.reciprocal(rs_sb[:, :], sqr_sb[:, :]))
        VS(nc.vector.tensor_scalar_mul(rs_sb[:, :], rs_sb[:, :], C(C_CS)))
        sVo.inc(nc.vector.tensor_tensor(
            obt4.ap().rearrange("p (g k) -> p g k", k=32),
            PSB.ap().rearrange("p (g k) -> p g k", k=32),
            rs_sb.ap().unsqueeze(2).broadcast_to([128, 32, 32]),
            ALU.mult))

        def pe_bwd(t=t):
            W(nc.tensor, sVo, t + 1)
            for qs in range(4):
                nc.tensor.transpose(PSC[:, qs * 256:qs * 256 + 128],
                                    obt4[:, qs * 256:qs * 256 + 128], idf[:, :])
                mm = nc.tensor.transpose(PSC[:, qs * 256 + 128:qs * 256 + 256],
                                         obt4[:, qs * 256 + 128:qs * 256 + 256], idf[:, :])
            sTb.inc(mm)

        W(nc.vector, sTb, t + 1)
        W(nc.vector, sGo, 128 * t)
        sVf.inc(nc.vector.tensor_copy(obf4[:, :], PSC[:, :]))
        W(nc.gpsimd, sVf, t + 1)
        for qs in range(4):
            b = t * 4 + qs
            shard, col = b % 8, (b // 8) * 128
            for a in range(2):
                sGo.inc(nc.gpsimd.dma_start(
                    out=cc_in[shard, a * 128:(a + 1) * 128, col:col + 128],
                    in_=obf4[:, qs * 256 + a * 128:qs * 256 + (a + 1) * 128]), 16)
        if t + 1 < QT:
            epi = {"rbc": pe_rbc, "fwd": pe_fwd, "sq": sc_sq, "sr": sc_sr,
                   "bwd": pe_bwd}
        else:
            pe_rbc()
            pe_fwd()
            sc_sq()
            sc_sr()
            pe_bwd()

    for i in range(4 * QT, 64):
        emit_wot(i)
    nc.scalar.wait_ge(sWoC.h, 64)
    nc.gpsimd.wait_ge(sGo.h, sGo.n)

    stB.close()
    bar()
    sCC = Sem("sCC")
    with allow_ici_all_to_all():
        sCC.inc(nc.gpsimd.collective_compute(
            "AllToAll", ALU.bypass, replica_groups=[list(range(8))],
            ins=[cc_in.ap().opt()], outs=[cc_out.ap().opt()]))
    nc.gpsimd.wait_ge(sCC.h, 1)
    bar()
    stD = ExitStack()
    ot_sb = stD.enter_context(nc.sbuf_tensor("ot_sb", [128, 2, 16, 128], BF16))
    x_sb = stD.enter_context(nc.sbuf_tensor("x_sb", [128, 2, D], F32))
    z_sb = stD.enter_context(nc.sbuf_tensor("z_sb", [128, D], F32))
    y_sb = stD.enter_context(nc.sbuf_tensor("y_sb", [128, D], F32))
    yT_sb = stD.enter_context(nc.sbuf_tensor("yT_sb", [128, 16, TPC], BF16))
    simn = stD.enter_context(nc.sbuf_tensor("simn", [128, 16, 16], F32))
    gb_sb = stD.enter_context(nc.sbuf_tensor("gb_sb", [128, 16], F32))
    rt = stD.enter_context(nc.sbuf_tensor("rt", [128, 160], F32))
    stg_big = stD.enter_context(nc.sbuf_tensor("stg_big", [128, 8, 512], F32))
    yTf_sb = stg_big.ap().rearrange("p a b -> p (a b)")[:, 0:2048]\
        .rearrange("p (a b) -> p a b", a=16)
    acc_q = stD.enter_context(nc.sbuf_tensor("acc_q", [128, MY, 128], F32))
    acc_k = stD.enter_context(nc.sbuf_tensor("acc_k", [128, MY, 128], F32))
    acc_v = stD.enter_context(nc.sbuf_tensor("acc_v", [128, MY, 128], F32))
    wp_bf = stD.enter_context(nc.sbuf_tensor("wp_bf", [128, 2, 16, 512], BF16))

    for dc in range(16):
        dma_b(simn[:, dc, :], simm[dc, :, :])
    dma_b(rt[0:1, 0:16], gates[:, :])
    dma_b_wait()
    bar()
    with nc.sbuf_tensor("simsq", [128, 16, 16], F32) as simsq:
        nc.vector.tensor_mul(simsq[:, :, :], simn[:, :, :], simn[:, :, :])
        bar()
        for dc in range(16):
            nc.tensor.matmul(PSD[0:1, 0:16], onesc[:, :], simsq[:, dc, :],
                             start=(dc == 0), stop=(dc == 15))
        bar()
    nc.vector.tensor_copy(rt[0:1, 48:64], PSD[0:1, 0:16])
    bar()
    nc.scalar.activation(rt[0:1, 64:80], rt[0:1, 48:64], AF.Sqrt)
    bar()
    nc.vector.reciprocal(rt[0:1, 80:96], rt[0:1, 64:80])
    nc.scalar.activation(rt[0:1, 96:112], rt[0:1, 0:16], AF.Sigmoid)
    bar()
    nc.tensor.matmul(PSD[:, 16:32], onesf[:, :], rt[0:1, 80:96], start=True, stop=True)
    nc.tensor.matmul(PSD[:, 32:48], onesf[:, :], rt[0:1, 96:112], start=True, stop=True)
    bar()
    nc.vector.tensor_copy(rt[:, 112:128], PSD[:, 16:32])
    nc.vector.tensor_copy(gb_sb[:, :], PSD[:, 32:48])
    bar()
    nc.vector.tensor_tensor(
        simn[:, :, :], simn[:, :, :],
        rt[:, 112:128].unsqueeze(1).broadcast_to([128, 16, 16]), ALU.mult)
    bar()

    sQo = [Sem("sQo0"), Sem("sQo1")]
    sQx = [Sem("sQx0"), Sem("sQx1")]
    sTwo = Sem("sTwo")
    zval = {}
    d1_pending = None
    for m in range(MY):
        if m == 0:
            for oc in range(16):
                sQo[0].inc(nc.sync.dma_start(
                    out=ot_sb[:, 0, oc, :],
                    in_=cc_out[oc // 2, (oc % 2) * 128:(oc % 2 + 1) * 128, 0:128]), 16)
            sQx[0].inc(nc.sync.dma_start(out=x_sb[:, 0, :], in_=x_rows[0, :, :]), 16)
        if m + 1 < MY:
            W(nc.sync, sTwo, m)
            for oc in range(16):
                sQo[(m + 1) % 2].inc(nc.sync.dma_start(
                    out=ot_sb[:, (m + 1) % 2, oc, :],
                    in_=cc_out[oc // 2, (oc % 2) * 128:(oc % 2 + 1) * 128,
                               (m + 1) * 128:(m + 2) * 128]), 16)
            if m >= 1:
                W(nc.sync, sCH, zval[m - 1])
            sQx[(m + 1) % 2].inc(nc.sync.dma_start(
                out=x_sb[:, (m + 1) % 2, :], in_=x_rows[m + 1, :, :]), 16)
        W(nc.tensor, sQo[m % 2], 256 * (m // 2 + 1))
        if m >= 1:
            W(nc.tensor, sCH, zval[m - 1])
        mm = None
        for oc in range(16):
            for cg in range(4):
                PS = PSA if cg < 2 else PSB
                mm = nc.tensor.matmul(
                    PS[:, (cg % 2) * 512:(cg % 2 + 1) * 512],
                    ot_sb[:, m % 2, oc, :],
                    woT_bf[:, oc, cg * 512:(cg + 1) * 512],
                    start=(oc == 0), stop=(oc == 15))
        sTwo.inc(mm)
        if d1_pending is not None:
            d1_pending()
            d1_pending = None
        W(nc.vector, sTwo, m + 1)
        W(nc.vector, sQx[m % 2], 16 * (m // 2 + 1))
        CI(nc.vector.tensor_add(z_sb[:, 0:1024], PSA[:, :], x_sb[:, m % 2, 0:1024]))
        CI(nc.vector.tensor_add(z_sb[:, 1024:2048], PSB[:, :],
                                x_sb[:, m % 2, 1024:2048]))
        zval[m] = sCH.n
        CW(nc.vector)
        for cch in range(4):
            CI(nc.vector.bn_stats(rt[:, 128 + cch * 6:134 + cch * 6],
                                  z_sb[:, cch * 512:(cch + 1) * 512]))
        CW(nc.vector)
        CI(nc.vector.bn_aggr(rt[:, 152:154], rt[:, 128:152]))
        CW(nc.scalar)
        CI(nc.scalar.activation(rt[:, 154:155], rt[:, 153:154], AF.Sqrt,
                                bias=C(C_EPS5)))
        CI(nc.scalar.activation(rt[:, 155:156], rt[:, 153:154], AF.Sqrt))
        CW(nc.vector)
        CI(nc.vector.reciprocal(rt[:, 156:157], rt[:, 154:155]))
        CW(nc.vector)
        CI(nc.vector.tensor_mul(rt[:, 157:158], rt[:, 152:153], rt[:, 156:157]))
        CI(nc.vector.tensor_mul(rt[:, 158:159], rt[:, 155:156], rt[:, 156:157]))
        CW(nc.vector)
        CI(nc.vector.tensor_scalar_mul(rt[:, 157:158], rt[:, 157:158], C(C_M1)))
        CI(nc.vector.tensor_scalar_mul(rt[:, 158:159], rt[:, 158:159], C(C_SQ2048)))
        CW(nc.vector)
        CI(nc.vector.tensor_scalar_max(rt[:, 158:159], rt[:, 158:159], C(C_EPS12)))
        CW(nc.vector)
        CI(nc.vector.reciprocal(rt[:, 159:160], rt[:, 158:159]))
        W(nc.vector, sG, sG.n)
        CW(nc.vector)
        CI(nc.vector.tensor_scalar(y_sb[:, :], z_sb[:, :], rt[:, 156:157],
                                   rt[:, 157:158], ALU.mult, ALU.add))
        CW(nc.gpsimd)
        gdma_b(y_hbm[m, :, :], y_sb[:, :])

        def pe_d1(m):
            CW(nc.tensor)
            for g4 in range(4):
                hb = (g4 % 2) * 512
                for i in range(4):
                    dc = g4 * 4 + i
                    mm = nc.tensor.transpose(
                        PSC[:, hb + i * 128:hb + (i + 1) * 128],
                        y_sb[:, dc * 128:(dc + 1) * 128], idf[:, :])
                CI(mm)
                CW(nc.vector)
                CI(nc.vector.tensor_copy(
                    yT_sb[:, g4 * 4:(g4 + 1) * 4, m * 128:(m + 1) * 128],
                    PSC[:, hb:hb + 512].rearrange("p (a q) -> p a q", a=4)))
                CI(nc.vector.tensor_copy(
                    yTf_sb[:, g4 * 4:(g4 + 1) * 4, :],
                    PSC[:, hb:hb + 512].rearrange("p (a q) -> p a q", a=4)))
                if g4 + 1 < 4:
                    CW(nc.tensor)
            CW(nc.tensor)
            for dc in range(16):
                mm = nc.tensor.matmul(PSD[:, 0:16], yTf_sb[:, dc, :],
                                      simn[:, dc, :],
                                      start=(dc == 0), stop=(dc == 15))
            CI(mm)

        def d1_tail(m=m):
            pe_d1(m)
            v = nc.vector
            CW(v)
            CI(v.tensor_scalar_mul(rt[:, 0:16], PSD[:, 0:16], rt[:, 159:160]))
            CW(v)
            CI(v.tensor_sub(rt[:, 0:16], rt[:, 0:16], gb_sb[:, :]))
            CW(v)
            CI(v.tensor_single_scalar(rt[:, 16:32], rt[:, 0:16], 0.0, ALU.is_gt))
            CW(v)
            CI(v.reduce_sum(rt[:, 32:33], rt[:, 16:32], AX.X))
            CI(v.reduce_max(rt[:, 33:34], rt[:, 0:16], AX.X))
            CW(v)
            CI(v.tensor_single_scalar(rt[:, 34:35], rt[:, 32:33], 0.0, ALU.is_equal))
            CI(v.tensor_scalar(rt[:, 48:64], rt[:, 0:16], rt[:, 33:34], None, ALU.is_ge))
            CW(v)
            CI(v.scalar_tensor_tensor(rt[:, 64:80], rt[:, 48:64], C(C_M1E9), rt[:, 0:16],
                                      ALU.mult, ALU.add))
            CW(v)
            CI(v.reduce_max(rt[:, 35:36], rt[:, 64:80], AX.X))
            CW(v)
            CI(v.tensor_scalar(rt[:, 80:96], rt[:, 0:16], rt[:, 35:36], None, ALU.is_ge))
            CW(v)
            CI(v.tensor_scalar_mul(rt[:, 80:96], rt[:, 80:96], rt[:, 34:35]))
            CW(v)
            CI(v.tensor_max(rt[:, 16:32], rt[:, 16:32], rt[:, 80:96]))
            CI(v.tensor_scalar_max(rt[:, 96:112], rt[:, 0:16], 0.0))
            CW(v)
            CI(v.tensor_scalar(rt[:, 96:112], rt[:, 96:112], C(C_1E9), None, ALU.add))
            CW(v)
            CI(v.tensor_mul(rt[:, 96:112], rt[:, 96:112], rt[:, 16:32]))
            CW(v)
            CI(v.tensor_scalar(rt[:, 96:112], rt[:, 96:112], C(C_M1E9), None, ALU.add))
            CW(nc.scalar)
            CI(nc.scalar.activation(rt[:, 112:128], rt[:, 96:112], AF.Exp))
            CW(v)
            CI(v.reduce_sum(rt[:, 36:37], rt[:, 112:128], AX.X))
            CW(v)
            CI(v.reciprocal(rt[:, 37:38], rt[:, 36:37]))
            CW(v)
            CI(v.tensor_scalar_mul(w_sb[:, m, :], rt[:, 112:128], rt[:, 37:38]))

        if m + 1 < MY:
            d1_pending = d1_tail
        else:
            d1_tail()
    bar()
    sWp = Sem("sWp")
    sTp = Sem("sTp")
    sVp = Sem("sVp")
    sDW = [Sem("sDW0"), Sem("sDW1")]
    for ip, (wsrc, acc) in enumerate([(qall, acc_q), (kall, acc_k), (vall, acc_v)]):
        for pn in range(4):
            pg = ip * 4 + pn
            for hh in range(2):
                ii = pg * 2 + hh
                W(nc.sync, sWp, ii)
                if ii >= 1:
                    W(nc.sync, sDW[(ii - 1) % 2], 16 * ((ii - 1) // 2 + 1))
                sDW[ii % 2].inc(nc.sync.dma_start(
                    out=stg_big[:, :, :],
                    in_=wsrc[:, pn, hh * 8:(hh + 1) * 8, :]), 16)
                W(nc.scalar, sDW[ii % 2], 16 * (ii // 2 + 1))
                if pg >= 2:
                    W(nc.scalar, sTp, MY * (pg - 1))
                sWp.inc(nc.scalar.copy(
                    wp_bf.ap().rearrange("p s a b -> p s (a b)")
                    [:, pg % 2, hh * 4096:(hh + 1) * 4096],
                    stg_big.ap().rearrange("p a b -> p (a b)")))
            for m in range(MY):
                tm = pg * MY + m
                W(nc.tensor, sWp, 2 * (pg + 1))
                W(nc.tensor, sVp, tm - 1)
                PS = PSD if tm % 2 == 0 else PSP
                mmi = None
                for dc in range(16):
                    mmi = nc.tensor.matmul(
                        PS[:, 0:512], yT_sb[:, dc, m * 128:(m + 1) * 128],
                        wp_bf[:, pg % 2, dc, :], start=(dc == 0), stop=(dc == 15))
                sTp.inc(mmi)
                W(nc.vector, sTp, tm + 1)
                for el in range(4):
                    e = pn * 4 + el
                    if pn == 0 and el == 0:
                        iv = nc.vector.tensor_scalar_mul(
                            acc[:, m, :], PS[:, el * 128:(el + 1) * 128],
                            w_sb[:, m, e:e + 1])
                    else:
                        iv = nc.vector.scalar_tensor_tensor(
                            acc[:, m, :], PS[:, el * 128:(el + 1) * 128],
                            w_sb[:, m, e:e + 1], acc[:, m, :], ALU.mult, ALU.add)
                    if el < 3:
                        VS(iv)
                sVp.inc(iv)
    bar()
    for m in range(MY):
        hb = (m % 2) * 512
        nc.tensor.transpose(PSC[:, hb:hb + 128], acc_q[:, m, :], idf[:, :])
        nc.tensor.transpose(PSC[:, hb + 128:hb + 256], acc_k[:, m, :], idf[:, :])
        bar()
        nc.vector.tensor_copy(qTf[:, m, :], PSC[:, hb:hb + 128])
        nc.vector.tensor_copy(ot_sb[:, 0, 0, :], PSC[:, hb + 128:hb + 256])
        nc.vector.tensor_copy(ot_sb[:, 0, 1, :], acc_v[:, m, :])
        bar()
        gdma_b(cc2_in[0, :, m * 128:(m + 1) * 128], ot_sb[:, 0, 0, :])
        gdma_b(cc2_in[1, :, m * 128:(m + 1) * 128], ot_sb[:, 0, 1, :])
        gdma_b_wait()
        bar()
    stD.close()
    bar()
    sCC.inc(nc.gpsimd.collective_compute(
        "AllGather", ALU.bypass, replica_groups=[list(range(8))],
        ins=[cc2_in.ap().opt()], outs=[cc2_out.ap().opt()]))
    nc.gpsimd.wait_ge(sCC.h, 2)
    bar()
    stF = ExitStack()
    kTa = stF.enter_context(nc.sbuf_tensor("kTa", [128, 8, MY * 128], BF16))
    va = stF.enter_context(nc.sbuf_tensor("va", [128, 8, MY * 128], BF16))
    fm_sb = stF.enter_context(nc.sbuf_tensor("fm_sb", [128, 1024], BF16))
    pf_sb = stF.enter_context(nc.sbuf_tensor("pf_sb", [128, 2, 512], BF16))
    rr2 = stF.enter_context(nc.sbuf_tensor("rr2", [1, 128], F32))
    rbc2 = stF.enter_context(nc.sbuf_tensor("rbc2", [128, 128], F32))
    aoTn = stF.enter_context(nc.sbuf_tensor("aoTn", [128, 128], F32))
    wT_sb = stF.enter_context(nc.sbuf_tensor("wT_sb", [16, 128], F32))
    wTf = stF.enter_context(nc.sbuf_tensor("wTf", [1, 2048], BF16))
    onesfb = stF.enter_context(nc.sbuf_tensor("onesfb", [1, 128], BF16))
    Zt = stF.enter_context(nc.sbuf_tensor("Zt", [128, G, 16, 128], BF16))
    op_bf = stF.enter_context(nc.sbuf_tensor("op_bf", [128, 2, 16, 512], BF16))
    z2 = stF.enter_context(nc.sbuf_tensor("z2", [128, G, D], F32))
    y2p = stF.enter_context(nc.sbuf_tensor("y2p", [128, 2, 512], F32))
    stF_big = stF.enter_context(nc.sbuf_tensor("stF_big", [128, 4, 512], F32))
    rt2 = stF.enter_context(nc.sbuf_tensor("rt2", [128, 32], F32))

    for r in range(8):
        dma_b(kTa[:, r, :], cc2_out[r, 0, :, :])
        dma_b(va[:, r, :], cc2_out[r, 1, :, :])
    dma_b(fm_sb[:, :], fmask[:, :])
    dma_b_wait()
    bar()
    nc.vector.memset(onesfb[:, :], 1.0)
    bar()

    sT2 = Sem("sT2")
    sE2 = Sem("sE2")
    sM2 = Sem("sM2")
    sTt2 = Sem("sTt2")
    sOW = Sem("sOW")
    sTO = Sem("sTO")
    sVO = Sem("sVO")
    sFW = [Sem("sFW0"), Sem("sFW1")]
    sY2 = [Sem("sY20"), Sem("sY21")]

    GBC = 0
    ztval = {}
    for hg in range(NG):
        tiles = list(range(hg * G, (hg + 1) * G))
        for mi, m in enumerate(tiles):
            j = m
            ngr = 2 * (j + 1)
            m2count = {}

            def emit_pv2(gg, ngr=ngr):
                first_g = (gg == 0)
                last_g = (gg == ngr - 1)
                if gg >= ngr - 2:
                    W(nc.tensor, sM2, m2count[gg])
                else:
                    W(nc.tensor, sE2, GBC + gg + 1)
                if first_g:
                    W(nc.tensor, sVO, sVO.n)
                for i in range(4):
                    a = gg * 4 + i
                    first = (a == 0)
                    last = (a == ngr * 4 - 1)
                    nc.tensor.matmul(
                        PSB[:, 0:128],
                        va[:, a % 8, (a // 8) * 128:(a // 8 + 1) * 128],
                        pf_sb[:, gg % 2, i * 128:(i + 1) * 128],
                        start=first, stop=last)
                    mm = nc.tensor.matmul(PSP[0:1, 0:128], onesb[:, :],
                                          pf_sb[:, gg % 2, i * 128:(i + 1) * 128],
                                          start=first, stop=last)
                    if last:
                        sTt2.inc(mm)

            for g in range(ngr):
                gi = GBC + g
                W(nc.tensor, sE2, gi - 1)
                ha = (g % 2) * 512
                mm = None
                for i in range(4):
                    a = g * 4 + i
                    mm = nc.tensor.matmul(
                        PSA[:, ha + i * 128:ha + (i + 1) * 128],
                        kTa[:, a % 8, (a // 8) * 128:(a // 8 + 1) * 128], qTf[:, m, :],
                        start=True, stop=True)
                sT2.inc(mm)
                if g >= 1:
                    emit_pv2(g - 1)
                W(nc.scalar, sT2, gi + 1)
                sE2.inc(nc.scalar.activation(
                    pf_sb[:, g % 2, :], PSA[:, ha:ha + 512], AF.Exp,
                    bias=C(C_BF), scale=C(C_SF)))
                if g >= ngr - 2:
                    bi0 = (g - (ngr - 2)) * 4
                    W(nc.vector, sE2, gi + 1)
                    sM2.inc(nc.vector.tensor_mul(
                        pf_sb[:, g % 2, :], pf_sb[:, g % 2, :],
                        fm_sb[:, bi0 * 128:(bi0 + 4) * 128]))
                    m2count[g] = sM2.n
            emit_pv2(ngr - 1)
            GBC += ngr
            W(nc.vector, sTt2, sTt2.n)
            CI(nc.vector.tensor_copy(rr2[0:1, :], PSP[0:1, 0:128]))
            CW(nc.vector)
            CI(nc.vector.reciprocal(rr2[:, :], rr2[:, :]))
            CW(nc.tensor)
            CI(nc.tensor.matmul(PSC[:, 0:128], onesf[:, :], rr2[0:1, :],
                                start=True, stop=True))
            CI(nc.tensor.transpose(PSC[0:16, 128:256], w_sb[:, m, :], idf[:, :]))
            CW(nc.vector)
            CI(nc.vector.tensor_copy(rbc2[:, :], PSC[:, 0:128]))
            W(nc.vector, sG, sG.n)
            CI(nc.vector.tensor_copy(wT_sb[:, :], PSC[0:16, 128:256]))
            CW(nc.gpsimd)
            gdma_b(wTf[0:1, :], wT_sb[:, :])
            CW(nc.vector)
            CI(nc.vector.tensor_mul(aoTn[:, :], PSB[:, 0:128], rbc2[:, :]))
            pvfree = sCH.n
            W(nc.tensor, sG, sG.n)
            CW(nc.tensor)
            for eg in range(4):
                for el in range(4):
                    e = eg * 4 + el
                    mm = nc.tensor.matmul(
                        PSC[:, 512 + el * 128:512 + (el + 1) * 128],
                        onesfb[:, :], wTf[0:1, e * 128:(e + 1) * 128],
                        start=True, stop=True)
                CI(mm)
                CW(nc.vector)
                CI(nc.vector.tensor_tensor(
                    Zt[:, mi, eg * 4:(eg + 1) * 4, :],
                    aoTn.ap().unsqueeze(1).broadcast_to([128, 4, 128]),
                    PSC[:, 512:1024].rearrange("p (a q) -> p a q", a=4),
                    ALU.mult))
                if eg + 1 < 4:
                    CW(nc.tensor)
            ztval[mi] = sCH.n
        for pn in range(4):
            pp = hg * 4 + pn
            for hh in range(4):
                ii = pp * 4 + hh
                W(nc.sync, sOW, ii)
                if ii >= 1:
                    W(nc.sync, sFW[(ii - 1) % 2], 16 * ((ii - 1) // 2 + 1))
                sFW[ii % 2].inc(nc.sync.dma_start(
                    out=stF_big[:, :, :], in_=oall[:, pn, hh * 4:(hh + 1) * 4, :]), 16)
                W(nc.scalar, sFW[ii % 2], 16 * (ii // 2 + 1))
                if pp >= 2:
                    W(nc.scalar, sTO, G * (pp - 1))
                sOW.inc(nc.scalar.copy(
                    op_bf.ap().rearrange("p s a b -> p s (a b)")
                    [:, pp % 2, hh * 2048:(hh + 1) * 2048],
                    stF_big.ap().rearrange("p a b -> p (a b)")))
            for mi, m in enumerate(tiles):
                tm2 = pp * G + mi
                W(nc.sync, sVO, tm2 - 1)
                sY2[tm2 % 2].inc(nc.sync.dma_start(
                    out=y2p[:, tm2 % 2, :],
                    in_=y_hbm[m, :, pn * 512:(pn + 1) * 512]), 16)
                W(nc.tensor, sOW, 4 * (pp + 1))
                W(nc.tensor, sVO, tm2 - 1)
                if pn == 0:
                    W(nc.tensor, sCH, ztval[mi])
                PS = PSD if tm2 % 2 == 0 else PSP
                mo = None
                for e in range(16):
                    mo = nc.tensor.matmul(
                        PS[:, 0:512], Zt[:, mi, e, :], op_bf[:, pp % 2, e, :],
                        start=(e == 0), stop=(e == 15))
                sTO.inc(mo)
                W(nc.vector, sTO, tm2 + 1)
                W(nc.vector, sY2[tm2 % 2], 16 * (tm2 // 2 + 1))
                if pn == 0 and mi == 0:
                    W(nc.vector, sG, sG.n)
                sVO.inc(nc.vector.tensor_add(
                    z2[:, mi, pn * 512:(pn + 1) * 512], PS[:, 0:512],
                    y2p[:, tm2 % 2, :]))
        W(nc.vector, sVO, 4 * G * (hg + 1))
        for mi, m in enumerate(tiles):
            for cch in range(4):
                CI(nc.vector.bn_stats(rt2[:, cch * 6:6 + cch * 6],
                                      z2[:, mi, cch * 512:(cch + 1) * 512]))
            CW(nc.vector)
            CI(nc.vector.bn_aggr(rt2[:, 24:26], rt2[:, 0:24]))
            CW(nc.scalar)
            CI(nc.scalar.activation(rt2[:, 26:27], rt2[:, 25:26], AF.Sqrt,
                                    bias=C(C_EPS5)))
            CW(nc.vector)
            CI(nc.vector.reciprocal(rt2[:, 27:28], rt2[:, 26:27]))
            CW(nc.vector)
            CI(nc.vector.tensor_mul(rt2[:, 28:29], rt2[:, 24:25], rt2[:, 27:28]))
            CW(nc.vector)
            CI(nc.vector.tensor_scalar_mul(rt2[:, 28:29], rt2[:, 28:29], C(C_M1)))
            CW(nc.vector)
            CI(nc.vector.tensor_scalar(z2[:, mi, :], z2[:, mi, :], rt2[:, 27:28],
                                       rt2[:, 28:29], ALU.mult, ALU.add))
            CW(nc.gpsimd)
            gdma_b(out_rows[m, :, :], z2[:, mi, :])
    gdma_b_wait()
    stF.close()
    bar()
    st_base.close()
    return nc


def make_in_maps(inp, S):
    MY = (S // 128) // 8
    x = np.ascontiguousarray(np.asarray(inp["L0_attn_x"], dtype=np.float32).reshape(S, D))
    wq = np.asarray(inp["L0_attn_wq"], dtype=np.float32)
    wk = np.asarray(inp["L0_attn_wk"], dtype=np.float32)
    wv = np.asarray(inp["L0_attn_wv"], dtype=np.float32)
    wo = np.asarray(inp["L0_attn_wo"], dtype=np.float32)
    sim = np.asarray(inp["L0_ffn_sim_matrix"], dtype=np.float32)
    gates = np.asarray(inp["L0_ffn_gates"], dtype=np.float32)
    qp = np.asarray(inp["L0_ffn_q_proj"], dtype=np.float32)
    kp = np.asarray(inp["L0_ffn_k_proj"], dtype=np.float32)
    vp = np.asarray(inp["L0_ffn_v_proj"], dtype=np.float32)
    op = np.asarray(inp["L0_ffn_o_proj"], dtype=np.float32)

    xT = np.ascontiguousarray(x.T).reshape(16, 128, S)
    woT = np.ascontiguousarray(wo.T).reshape(16, 128, D)
    simm = np.ascontiguousarray(sim).reshape(16, 128, 16)
    def _pm(w):
        return np.ascontiguousarray(
            w.reshape(16, 128, 4, 512).transpose(1, 2, 0, 3))
    qall = _pm(qp.transpose(1, 0, 2).reshape(D, D))
    kall = _pm(kp.transpose(1, 0, 2).reshape(D, D))
    vall = _pm(vp.transpose(1, 0, 2).reshape(D, D))
    oall = _pm(op.reshape(D, D))
    lvv = np.ascontiguousarray(np.concatenate([
        np.asarray(inp["L0_attn_l_q1"], dtype=np.float32),
        np.asarray(inp["L0_attn_l_k1"], dtype=np.float32),
        np.asarray(inp["L0_attn_l_q2"], dtype=np.float32),
        np.asarray(inp["L0_attn_l_k2"], dtype=np.float32)]).reshape(1, 32))
    gatesr = np.ascontiguousarray(gates.reshape(1, 16))

    in_maps = []
    for c in range(8):
        h, half = c // 2, c % 2
        r1 = h * 512 + half * 256
        r2 = (h + 4) * 512 + half * 256
        wqk_c = np.concatenate([
            wq[r1:r1 + 256].T, wq[r2:r2 + 256].T,
            wk[r1:r1 + 256].T, wk[r2:r2 + 256].T], axis=1)
        wqk_c = np.ascontiguousarray(wqk_c).reshape(16, 128, 1024)
        wv_c = np.ascontiguousarray(wv[c * 256:(c + 1) * 256].T).reshape(16, 128, 256)
        x_rows_c = np.ascontiguousarray(np.stack(
            [x[(8 * j + c) * 128:(8 * j + c + 1) * 128] for j in range(MY)]))
        p = np.arange(128)[:, None]
        q = np.arange(128)[None, :]
        fm = np.zeros((128, 8, 128), dtype=np.float32)
        for bi in range(8):
            fm[:, bi, :] = ((bi - c) * 128 + p <= q)
        fmask_c = np.ascontiguousarray(fm.reshape(128, 1024)).astype(ml_dtypes.bfloat16)
        in_maps.append({
            "xT": xT, "x_rows": x_rows_c, "wqk": wqk_c, "wv": wv_c, "lv": lvv,
            "woT": woT, "simm": simm, "gates": gatesr, "qall": qall,
            "kall": kall, "vall": vall, "oall": oall, "fmask": fmask_c,
        })
    return in_maps


def assemble(results, S):
    MY = (S // 128) // 8
    out = np.empty((S, D), dtype=np.float32)
    for c in range(8):
        o = np.asarray(results[c]["out_rows"], dtype=np.float32).reshape(MY, 128, D)
        for j in range(MY):
            out[(8 * j + c) * 128:(8 * j + c + 1) * 128] = o[j]
    return out.reshape(1, S, D)


def kernel(**inp):
    S = 8192
    nc = build(S)
    in_maps = make_in_maps(inp, S)
    res = run_bass_kernel_spmd(nc, in_maps, core_ids=list(range(8)))
    return assemble(res.results, S)
